# version 06
# baseline (speedup 1.0000x reference)
import numpy as np
import concourse.bass as bass
import concourse.mybir as mybir
from concourse.bass_utils import run_bass_kernel_spmd

F32 = mybir.dt.float32
BF16 = mybir.dt.bfloat16
AF = mybir.ActivationFunctionType
ALU = mybir.AluOpType
AX = mybir.AxisListType

NCORES = 8
B_LOC = 4
T = 2048
C = 1024
HID = 4096
NPT = C // 128
NHT = HID // 128
EPS = 1e-5
M32 = 12582912.0
M52 = 1.5 * 2.0 ** 52

STAGES = "a1,a2a,a2b,b1,b2,c,d"
DBG = ""
B1_SRC = "x1"
B1_WPREP = True

ENGINES = ("tensor", "vector", "scalar", "gpsimd", "sync")


class Buf:
    __slots__ = ("name", "h", "last_writer", "readers")

    def __init__(self, name, handle):
        self.name = name
        self.h = handle
        self.last_writer = None
        self.readers = []

    def __getitem__(self, idx):
        return self.h[idx]

    @property
    def ap(self):
        return self.h[:]


class Op:
    __slots__ = ("idx", "eng", "fn", "reads", "writes", "deps", "is_dma",
                 "need_inc", "tag")

    def __init__(self, idx, eng, fn, reads, writes, is_dma, tag):
        self.idx = idx
        self.eng = eng
        self.fn = fn
        self.reads = reads
        self.writes = writes
        self.is_dma = is_dma
        self.deps = set()
        self.need_inc = False
        self.tag = tag


class Prog:

    def __init__(self, nc):
        self.nc = nc
        self.ops = []
        self.sem_c = {e: nc.alloc_semaphore(f"semc_{e}") for e in ENGINES}
        self.sem_d = {e: nc.alloc_semaphore(f"semd_{e}") for e in ENGINES}
        self.last_by_eng = {e: None for e in ENGINES}
        self.fence_ops = ()
        self.store_ops = []

    def op(self, eng, fn, reads=(), writes=(), is_dma=False, tag="",
           extra_deps=(), is_store=False):
        o = Op(len(self.ops), eng, fn, tuple(reads), tuple(writes), is_dma, tag)
        for b in o.reads:
            if b.last_writer is not None:
                o.deps.add(b.last_writer)
            b.readers.append(o.idx)
        for b in o.writes:
            if b.last_writer is not None:
                o.deps.add(b.last_writer)
            for r in b.readers:
                o.deps.add(r)
            b.last_writer = o.idx
            b.readers = []
        for x in extra_deps:
            if x is not None:
                o.deps.add(x if isinstance(x, int) else x.idx)
        for f in self.fence_ops:
            o.deps.add(f)
        o.deps.discard(o.idx)
        self.ops.append(o)
        self.last_by_eng[eng] = o.idx
        if is_store:
            self.store_ops.append(o.idx)
        return o

    def barrier(self):
        tails = [v for v in self.last_by_eng.values() if v is not None]
        bar = []
        for e in ENGINES:
            o = self.op(e, lambda eng: eng.nop(), extra_deps=tails,
                        tag="barrier")
            bar.append(o.idx)
        self.fence_ops = tuple(bar)

    def finish(self):
        self.op("sync", lambda eng: eng.nop(), extra_deps=list(self.store_ops),
                tag="final_store_wait")

    def build(self):
        ops = self.ops

        def needs_wait(prod, cons):
            if prod.eng != cons.eng:
                return True
            return prod.is_dma and not cons.is_dma

        for o in ops:
            if o.is_dma:
                o.need_inc = True
            for dx in o.deps:
                if needs_wait(ops[dx], o):
                    ops[dx].need_inc = True
        cnt = {e: {False: 0, True: 0} for e in ENGINES}
        val_after = [None] * len(ops)
        for o in ops:
            if o.need_inc:
                cnt[o.eng][o.is_dma] += 16 if o.is_dma else 1
            val_after[o.idx] = (cnt[o.eng][False], cnt[o.eng][True])
        streams = {e: [] for e in ENGINES}
        hw = {e: {(f, dm): 0 for f in ENGINES for dm in (False, True)}
              for e in ENGINES}
        for o in ops:
            waits = {}
            for dx in sorted(o.deps):
                pr = ops[dx]
                if not needs_wait(pr, o):
                    continue
                key = (pr.eng, pr.is_dma)
                v = val_after[pr.idx][1 if pr.is_dma else 0]
                if v > hw[o.eng][key]:
                    waits[key] = max(waits.get(key, 0), v)
            for key, v in waits.items():
                hw[o.eng][key] = v
            streams[o.eng].append((o, dict(waits)))
        self._streams = streams

    def dump(self, path):
        with open(path, "w") as f:
            for e in ENGINES:
                for o, waits in self._streams[e]:
                    f.write(f"{e:7s} {'DMA' if o.is_dma else '   '} "
                            f"inc={int(o.need_inc)} {o.tag:20s} waits={waits}\n")

    def emit(self):
        nc = self.nc
        with nc.Block("main") as blk:
            for ename in ENGINES:
                stream = self._streams[ename]
                sem_c, sem_d = self.sem_c, self.sem_d

                def make(ename=ename, stream=stream):
                    def run(eng):
                        last_writes = frozenset()
                        for o, waits in stream:
                            for (peng, pdma), v in waits.items():
                                eng.wait_ge(sem_d[peng] if pdma else sem_c[peng], v)
                            if ename in ("vector", "gpsimd"):
                                rd = set(id(b) for b in o.reads)
                                if rd & last_writes:
                                    eng.tensor_copy(self.hz[ename][1][:],
                                                    self.hz[ename][0][:])
                            inst = o.fn(eng)
                            if o.need_inc:
                                assert inst is not None, \
                                    f"op {o.tag} needs inc but returned None"
                                inst.then_inc(sem_d[ename] if o.is_dma
                                              else sem_c[ename],
                                              16 if o.is_dma else 1)
                            last_writes = frozenset(id(b) for b in o.writes)
                    return run

                getattr(blk, ename)(make())



def build_program():
    nc = bass.Bass("TRN2", target_bir_lowering=False)
    p = Prog(nc)

    dram = {}
    dbuf = {}

    def dram_t(name, shape, kind, dt=F32):
        dram[name] = nc.dram_tensor(name, shape, dt, kind=kind)
        dbuf[name] = Buf("d_" + name, dram[name])
        return dram[name]

    dram_t("x", [B_LOC, C, T], "ExternalInput")
    dram_t("out", [B_LOC, C, T], "ExternalOutput")
    for nm in ("ln1_s", "ln1_b", "ln2_s", "ln2_b", "tmk", "tmv", "tmr",
               "w_decay", "u_first", "f_tmk", "f_tmr", "bias1f"):
        dram_t(nm, [128, NPT], "ExternalInput")
    dram_t("conv_b", [128, NHT], "ExternalInput")
    dram_t("conv_w", [128, NHT, 3], "ExternalInput")
    for nm, shp in (("Wk", [C, C]), ("Wv", [C, C]), ("Wr", [C, C]),
                    ("Wo", [C, C]), ("fWk", [C, HID]), ("fWr", [C, C]),
                    ("fWv", [HID, C]), ("w1T", [C, HID]), ("w2T", [HID, C]),
                    ("kern", [C, C]), ("aT", [16, C]), ("lora_b", [16, C])):
        dram_t(nm, shp, "ExternalInput")
    for nm in ("y2", "x1", "x2", "oc", "kv", "h2d"):
        dram_t(nm, [B_LOC, C, T], "Internal")
    dram_t("q2d", [B_LOC, HID, T], "Internal", BF16)
    dram_t("r4d", [T // 128 + 1, B_LOC, 129], "Internal")
    dram_t("rows", [8, B_LOC * T + 64], "Internal")
    dbuf["rows0"] = Buf("d_rows0", dram["rows"])
    dbuf["rows1"] = Buf("d_rows1", dram["rows"])
    dram_t("sca", [16, 1], "Internal")

    import contextlib
    _stack = [None]

    def sb(name, shape, dt=F32):
        if _stack[0] is None:
            return Buf(name, nc.alloc_sbuf_tensor(name, shape, dt))
        return Buf(name, _stack[0].enter_context(nc.sbuf_tensor(name, shape, dt)))

    def psb(name, shape, dt=F32):
        if _stack[0] is None:
            return Buf(name, nc.alloc_psum_tensor(name, shape, dt))
        return Buf(name, _stack[0].enter_context(nc.psum_tensor(name, shape, dt)))

    def stage_scope():
        return contextlib.ExitStack()

    ones_bf = sb("ones_bf", [128, 128], BF16)
    ident_bf = sb("ident_bf", [128, 128], BF16)
    ident_f = sb("ident_f", [128, 128], F32)
    magic = sb("magic", [128, 1], F32)
    iota_h = sb("iota_h", [128, 128], F32)
    iota_v = sb("iota_v", [128, 128], F32)

    p.op("gpsimd", lambda e: e.iota(iota_h.ap, [[1, 128]], channel_multiplier=0,
                                    allow_small_or_imprecise_dtypes=True),
         writes=(iota_h,), tag="iota_h")
    p.op("gpsimd", lambda e: e.iota(iota_v.ap, [[0, 128]], channel_multiplier=1,
                                    allow_small_or_imprecise_dtypes=True),
         writes=(iota_v,), tag="iota_v")
    iota_buf = iota_h
    p.hz = {
        "vector": (nc.alloc_sbuf_tensor("hzv_s", [128, 1], F32),
                   nc.alloc_sbuf_tensor("hzv_d", [128, 1], F32)),
        "gpsimd": (nc.alloc_sbuf_tensor("hzg_s", [128, 1], F32),
                   nc.alloc_sbuf_tensor("hzg_d", [128, 1], F32)),
    }

    def flat_ap(h):
        ap = h[:]
        nd = len(ap.shape)
        if nd == 2:
            return ap
        names = " ".join(chr(97 + i) for i in range(nd - 1))
        return ap.rearrange(f"p {names} -> p ({names})")

    def fill(buf, val, deps=()):
        ap = flat_ap(buf.h)
        n = ap.shape[1]
        p.op("vector",
             lambda e, ap=ap, n=n, val=val: e.tensor_scalar(
                 ap, iota_h[:, 0:1].broadcast_to((128, n)), 0.0, val,
                 ALU.mult, ALU.add),
             reads=(iota_buf,) + tuple(deps), writes=(buf,), tag="fill")

    magic_t = sb("magic_t", [128, 1])
    fill(magic, 1.0)
    p.op("vector", lambda e: e.tensor_scalar(magic.ap, magic.ap, 2.0 ** -40, 1.0,
                                             ALU.add, ALU.subtract),
         reads=(magic,), writes=(magic,))
    p.op("vector", lambda e: e.tensor_scalar(magic.ap, magic.ap, 2.0 ** 40, 0.0,
                                             ALU.mult, ALU.add),
         reads=(magic,), writes=(magic,))
    p.op("vector", lambda e: e.tensor_scalar(magic_t.ap, magic.ap, M52, 0.0,
                                             ALU.mult, ALU.add),
         reads=(magic,), writes=(magic_t,))
    p.op("vector", lambda e: e.tensor_scalar(magic.ap, magic.ap, -M32, M32,
                                             ALU.mult, ALU.add),
         reads=(magic,), writes=(magic,))
    p.op("vector", lambda e: e.tensor_tensor(magic.ap, magic.ap, magic_t.ap,
                                             ALU.add),
         reads=(magic, magic_t), writes=(magic,))

    p.op("vector", lambda e: e.tensor_tensor(ident_f.ap, iota_h.ap, iota_v.ap,
                                             ALU.is_equal),
         reads=(iota_h, iota_v), writes=(ident_f,), tag="identf")
    p.op("vector", lambda e: e.tensor_copy(ident_bf.ap, ident_f.ap),
         reads=(ident_f,), writes=(ident_bf,), tag="identbf")
    fill(ones_bf, 1.0)
    ones_f = sb("ones_f", [1, 128], F32)
    p.op("vector", lambda e: e.tensor_scalar(
        ones_f.ap, iota_h[0:1, 0:128], 0.0, 1.0, ALU.mult, ALU.add),
        reads=(iota_h,), writes=(ones_f,), tag="onesf")
    par = {}
    for nm in ("ln1_s", "ln1_b", "ln2_s", "ln2_b", "tmk", "tmv", "tmr",
               "w_decay", "u_first", "f_tmk", "f_tmr", "bias1f"):
        par[nm] = sb("p_" + nm, [128, NPT])
    par["conv_b"] = sb("p_conv_b", [128, NHT])
    par["conv_w"] = sb("p_conv_w", [128, NHT, 3])
    for nm, b in par.items():
        p.op("sync", lambda e, b=b, nm=nm: e.dma_start(b.ap, dram[nm][:]),
             reads=(dbuf[nm],), writes=(b,), is_dma=True, tag=f"ld_{nm}")
    par["dd"] = sb("p_dd", [128, NPT])
    par["eu"] = sb("p_eu", [128, NPT])
    p.op("scalar", lambda e: e.activation(par["dd"].ap, par["w_decay"].ap, AF.Exp),
         reads=(par["w_decay"],), writes=(par["dd"],))
    p.op("scalar", lambda e: e.activation(par["dd"].ap, par["dd"].ap, AF.Exp,
                                          scale=-1.0),
         reads=(par["dd"],), writes=(par["dd"],))
    p.op("scalar", lambda e: e.activation(par["eu"].ap, par["u_first"].ap, AF.Exp),
         reads=(par["eu"],) if False else (par["u_first"],), writes=(par["eu"],))

    def emit_ln_stats(xb, view, F, scr, skip_mean=False, npt=NPT, tag=""):
        inv_n = 1.0 / (128 * npt)
        if not skip_mean:
            for pt in range(npt):
                xbb = scr["xb0"] if pt % 2 == 0 else scr["xb1"]
                p.op("vector",
                     lambda e, pt=pt, xbb=xbb: e.tensor_copy(xbb[:, :F],
                                                             view(pt)),
                     reads=(xb,), writes=(xbb,), tag=f"{tag}cast{pt}")
                p.op("tensor",
                     lambda e, pt=pt, xbb=xbb: e.matmul(scr["psA"][:, :F],
                                               ones_bf.ap, xbb[:, :F],
                                               start=(pt == 0), stop=(pt == npt - 1)),
                     reads=(xbb, ones_bf), writes=(scr["psA"],),
                     tag=f"{tag}mmS{pt}")
        for pt in range(npt):
            sqb = scr["sq0"] if pt % 2 == 0 else scr["sq1"]
            p.op("scalar",
                 lambda e, pt=pt, sqb=sqb: e.activation(sqb[:, :F], view(pt),
                                                        AF.Square),
                 reads=(xb,), writes=(sqb,), tag=f"{tag}sq{pt}")
            p.op("tensor",
                 lambda e, pt=pt, sqb=sqb: e.matmul(scr["psB"][:, :F], ones_bf.ap,
                                                    sqb[:, :F],
                                                    start=(pt == 0), stop=(pt == npt - 1)),
                 reads=(sqb, ones_bf), writes=(scr["psB"],), tag=f"{tag}mmQ{pt}")
        if not skip_mean:
            p.op("vector", lambda e: e.tensor_scalar(scr["m_b"][:, :F],
                                                     scr["psA"][:, :F], inv_n,
                                                     None, ALU.mult),
                 reads=(scr["psA"],), writes=(scr["m_b"],), tag=f"{tag}mb")
        p.op("vector", lambda e: e.tensor_scalar(scr["v1"][:, :F],
                                                 scr["psB"][:, :F], inv_n, EPS,
                                                 ALU.mult, ALU.add),
             reads=(scr["psB"],), writes=(scr["v1"],), tag=f"{tag}v1")
        if not skip_mean:
            p.op("vector", lambda e: e.tensor_tensor(scr["r_b"][:, :F],
                                                     scr["m_b"][:, :F],
                                                     scr["m_b"][:, :F], ALU.mult),
                 reads=(scr["m_b"],), writes=(scr["r_b"],), tag=f"{tag}msq")
            p.op("vector", lambda e: e.tensor_tensor(scr["v1"][:, :F],
                                                     scr["v1"][:, :F],
                                                     scr["r_b"][:, :F],
                                                     ALU.subtract),
                 reads=(scr["v1"], scr["r_b"]), writes=(scr["v1"],),
                 tag=f"{tag}var")
        p.op("scalar", lambda e: e.activation(scr["v1"][:, :F], scr["v1"][:, :F],
                                              AF.Ln),
             reads=(scr["v1"],), writes=(scr["v1"],), tag=f"{tag}ln")
        p.op("scalar", lambda e: e.activation(scr["r_b"][:, :F], scr["v1"][:, :F],
                                              AF.Exp, scale=-0.5),
             reads=(scr["v1"],), writes=(scr["r_b"],), tag=f"{tag}rsq")

    def emit_a1(out_dram_name):
        Tc = 128
        F = B_LOC * Tc
        NCH = T // Tc
        wstage = sb("a1_wstage", [128, C])
        wts = {}
        for nm in ("Wk", "Wv", "Wr", "Wo"):
            wts[nm] = sb("a1_" + nm, [128, NPT, C], BF16)
            scale = 0.5 if nm == "Wo" else 1.0
            for kt in range(NPT):
                p.op("sync",
                     lambda e, nm=nm, kt=kt: e.dma_start(
                         wstage.ap, dram[nm][kt * 128:(kt + 1) * 128, :]),
                     reads=(dbuf[nm],), writes=(wstage,), is_dma=True,
                     tag=f"a1_ld{nm}{kt}")
                p.op("vector",
                     lambda e, nm=nm, kt=kt, scale=scale: e.tensor_scalar(
                         wts[nm][:, kt, :], wstage.ap, scale, None, ALU.mult),
                     reads=(wstage,), writes=(wts[nm],), tag=f"a1_cast{nm}{kt}")

        xt = sb("a1_xt", [128, 2, NPT, B_LOC, Tc])
        y = sb("a1_y", [128, NPT, B_LOC, Tc])
        h = sb("a1_h", [128, NPT, B_LOC, Tc], BF16)
        htail = sb("a1_htail", [128, NPT, B_LOC, 1], BF16)
        xkb = sb("a1_xkb", [128, NPT, B_LOC, Tc], BF16)
        xvb = sb("a1_xvb", [128, NPT, B_LOC, Tc], BF16)
        xrb = sb("a1_xrb", [128, NPT, B_LOC, Tc], BF16)
        ysr = sb("a1_ysr", [128, NPT, B_LOC, Tc], BF16)
        EK1 = sb("a1_EK1", [128, B_LOC, Tc])
        vv1 = sb("a1_vv1", [128, B_LOC, Tc])
        sig1 = sb("a1_sig1", [128, B_LOC, Tc])
        Pb = sb("a1_Pb", [128, B_LOC, Tc + 1])
        Qb = sb("a1_Qb", [128, B_LOC, Tc + 1])
        cP = sb("a1_cP", [128, NPT, B_LOC, 1])
        cQ = sb("a1_cQ", [128, NPT, B_LOC, 1])
        z3_t = sb("a1_z3", [128, Tc, 32]).h
        z3h = [Buf(f"a1_z3h{i}", z3_t) for i in range(2)]
        lifc_t = sb("a1_lifc", [128, 32]).h
        lifch = [Buf(f"a1_lifc{i}", lifc_t) for i in range(2)]
        scr = {
            "xb0": sb("a1_xb0", [128, F], BF16),
            "xb1": sb("a1_xb1", [128, F], BF16),
            "sq0": sb("a1_sq0", [128, F], BF16),
            "sq1": sb("a1_sq1", [128, F], BF16),
            "m_b": sb("a1_mb", [128, F]),
            "r_b": sb("a1_rb", [128, F]),
            "v1": sb("a1_v1", [128, F]),
        }
        psX = [psb(f"a1_psX{i}", [128, F]) for i in range(2)]
        scr["psA"], scr["psB"] = psX[0], psX[1]
        r1_b = sb("a1_r1b", [128, F])
        scr1 = dict(scr); scr1["r_b"] = r1_b
        psV = psb("a1_psV", [128, F])
        psKk = psb("a1_psKk", [128, F])
        psR = psb("a1_psR", [128, F])

        for b in (htail, cP, cQ):
            fill(b, 0.0)
        fill(lifch[0], 0.0)

        def fl(buf, pt):
            return buf.h[:, pt, :, :].rearrange("p b t -> p (b t)")

        def _store_dbg(ch, t0, buf):
            for pt in range(NPT):
                p.op("scalar",
                     lambda e, pt=pt, t0=t0: e.dma_start(
                         dram["out"][:, pt * 128:(pt + 1) * 128,
                                     t0:t0 + Tc].transpose([1, 0, 2]),
                         buf[:, pt, :, :]),
                     reads=(buf,), writes=(dbuf["out"],), is_dma=True,
                     is_store=True, tag=f"dbgst{ch}_{pt}")

        for ch in range(NCH):
            t0 = ch * Tc
            cur, nxt = ch % 2, (ch + 1) % 2
            if ch == 0:
                for pt in range(NPT):
                    p.op("sync",
                         lambda e, pt=pt, t0=t0, cur=cur: e.dma_start(
                             xt[:, cur, pt, :, :],
                             dram["x"][:, pt * 128:(pt + 1) * 128, t0:t0 + Tc]
                             .transpose([1, 0, 2])),
                         reads=(dbuf["x"],), writes=(xt,), is_dma=True,
                         tag=f"a1_ldx{ch}_{pt}")
            if ch + 1 < NCH:
                for pt in range(NPT):
                    p.op("sync",
                         lambda e, pt=pt, t1=t0 + Tc, nxt=nxt: e.dma_start(
                             xt[:, nxt, pt, :, :],
                             dram["x"][:, pt * 128:(pt + 1) * 128, t1:t1 + Tc]
                             .transpose([1, 0, 2])),
                         reads=(dbuf["x"],), writes=(xt,), is_dma=True,
                         tag=f"a1_ldx{ch + 1}_{pt}")

            xv_ = lambda pt, cur=cur: xt.h[:, cur, pt, :, :] \
                .rearrange("p b t -> p (b t)")
            emit_ln_stats(xt, xv_, F, scr, tag=f"a1c{ch}ln0_")
            for pt in range(NPT):
                p.op("vector",
                     lambda e, pt=pt, xv_=xv_: e.tensor_tensor(
                         fl(y, pt), xv_(pt), scr["m_b"].ap, ALU.subtract),
                     reads=(xt, scr["m_b"]), writes=(y,), tag=f"a1c{ch}sub{pt}")
            for pt in range(NPT):
                p.op("vector",
                     lambda e, pt=pt: e.tensor_tensor(
                         fl(y, pt), fl(y, pt), scr["r_b"].ap, ALU.mult),
                     reads=(y, scr["r_b"]), writes=(y,), tag=f"a1c{ch}nrm{pt}")
            emit_ln_stats(y, lambda pt: fl(y, pt), F, scr1, skip_mean=True,
                          tag=f"a1c{ch}ln1_")
            for pt in range(NPT):
                p.op("vector",
                     lambda e, pt=pt: e.tensor_tensor(
                         fl(h, pt), fl(y, pt), r1_b.ap, ALU.mult),
                     reads=(y, r1_b), writes=(h,), tag=f"a1c{ch}h1{pt}")
            for pt in range(NPT):
                p.op("scalar",
                     lambda e, pt=pt: e.activation(
                         fl(h, pt), fl(h, pt), AF.Identity,
                         bias=par["ln1_b"][:, pt:pt + 1],
                         scale=par["ln1_s"][:, pt:pt + 1]),
                     reads=(h, par["ln1_s"], par["ln1_b"]), writes=(h,),
                     tag=f"a1c{ch}aff{pt}")
            if DBG == "ln0":
                _store_dbg(ch, t0, y)
                continue
            if DBG == "h":
                for pt in range(NPT):
                    p.op("vector",
                         lambda e, pt=pt: e.tensor_copy(fl(y, pt), fl(h, pt)),
                         reads=(h,), writes=(y,), tag="dbgh")
                _store_dbg(ch, t0, y)
                continue
            for pt in range(NPT):
                p.op("vector",
                     lambda e, pt=pt: e.tensor_tensor(
                         ysr[:, pt, :, 1:], h[:, pt, :, 1:],
                         h[:, pt, :, 0:Tc - 1], ALU.subtract),
                     reads=(h,), writes=(ysr,), tag=f"a1c{ch}dl{pt}")
                p.op("vector",
                     lambda e, pt=pt: e.tensor_tensor(
                         ysr[:, pt, :, 0:1], h[:, pt, :, 0:1],
                         htail[:, pt, :, :], ALU.subtract),
                     reads=(h, htail), writes=(ysr,), tag=f"a1c{ch}dlb{pt}")
            for dst, pnm in ((xkb, "tmk"), (xvb, "tmv"), (xrb, "tmr")):
                for pt in range(NPT):
                    p.op("vector",
                         lambda e, pt=pt, pnm=pnm, dst=dst: e.scalar_tensor_tensor(
                             dst[:, pt, :, 1:], ysr[:, pt, :, 1:],
                             par[pnm][:, pt:pt + 1], h[:, pt, :, 0:Tc - 1],
                             ALU.mult, ALU.add),
                         reads=(ysr, h, par[pnm]), writes=(dst,),
                         tag=f"a1c{ch}m{pnm}{pt}")
                    p.op("vector",
                         lambda e, pt=pt, pnm=pnm, dst=dst: e.scalar_tensor_tensor(
                             dst[:, pt, :, 0:1], ysr[:, pt, :, 0:1],
                             par[pnm][:, pt:pt + 1], htail[:, pt, :, :],
                             ALU.mult, ALU.add),
                         reads=(ysr, htail, par[pnm]), writes=(dst,),
                         tag=f"a1c{ch}mb{pnm}{pt}")
            p.op("vector", lambda e: e.tensor_copy(htail.ap,
                                                   h[:, :, :, Tc - 1:Tc]),
                 reads=(h,), writes=(htail,), tag=f"a1c{ch}tail")

            if DBG == "mix":
                for pt in range(NPT):
                    p.op("vector",
                         lambda e, pt=pt: e.tensor_copy(fl(y, pt), fl(xkb, pt)),
                         reads=(xkb,), writes=(y,), tag="dbgm")
                _store_dbg(ch, t0, y)
                continue
            for pt in range(NPT):
                for kt in range(NPT):
                    p.op("tensor",
                         lambda e, pt=pt, kt=kt: e.matmul(
                             psV.ap, wts["Wv"][:, kt, pt * 128:(pt + 1) * 128],
                             fl(xvb, kt), start=(kt == 0), stop=(kt == NPT - 1)),
                         reads=(xvb, wts["Wv"]), writes=(psV,),
                         tag=f"a1c{ch}mmWv{pt}_{kt}")
                for kt in range(NPT):
                    p.op("tensor",
                         lambda e, pt=pt, kt=kt: e.matmul(
                             psKk.ap, wts["Wk"][:, kt, pt * 128:(pt + 1) * 128],
                             fl(xkb, kt), start=(kt == 0), stop=(kt == NPT - 1)),
                         reads=(xkb, wts["Wk"]), writes=(psKk,),
                         tag=f"a1c{ch}mmWk{pt}_{kt}")
                for kt in range(NPT):
                    p.op("tensor",
                         lambda e, pt=pt, kt=kt: e.matmul(
                             psR.ap, wts["Wr"][:, kt, pt * 128:(pt + 1) * 128],
                             fl(xrb, kt), start=(kt == 0), stop=(kt == NPT - 1)),
                         reads=(xrb, wts["Wr"]), writes=(psR,),
                         tag=f"a1c{ch}mmWr{pt}_{kt}")
                p.op("scalar",
                     lambda e: e.copy(vv1.h[:].rearrange("p b t -> p (b t)"),
                                      psV.ap),
                     reads=(psV,), writes=(vv1,), tag=f"a1c{ch}vv{pt}")
                p.op("scalar",
                     lambda e: e.activation(
                         EK1.h[:].rearrange("p b t -> p (b t)"), psKk.ap,
                         AF.Exp),
                     reads=(psKk,), writes=(EK1,), tag=f"a1c{ch}ek{pt}")
                p.op("scalar",
                     lambda e: e.activation(
                         sig1.h[:].rearrange("p b t -> p (b t)"), psR.ap,
                         AF.Exp, scale=-1.0),
                     reads=(psR,), writes=(sig1,), tag=f"a1c{ch}sg{pt}")
                if DBG == "kvr":
                    p.op("vector",
                         lambda e, pt=pt: e.tensor_copy(fl(y, pt),
                             EK1.h[:].rearrange("p b t -> p (b t)")),
                         reads=(EK1,), writes=(y,), tag="dbgk")
                    continue
                p.op("vector",
                     lambda e: e.tensor_scalar(sig1.ap, sig1.ap, 1.0, None,
                                               ALU.add),
                     reads=(sig1,), writes=(sig1,), tag=f"a1c{ch}s1{pt}")
                p.op("vector",
                     lambda e: e.reciprocal(sig1.ap, sig1.ap),
                     reads=(sig1,), writes=(sig1,), tag=f"a1c{ch}sr{pt}")
                p.op("vector",
                     lambda e: e.tensor_tensor(vv1.ap, EK1.ap, vv1.ap, ALU.mult),
                     reads=(EK1, vv1), writes=(vv1,), tag=f"a1c{ch}ekv{pt}")
                for b in range(B_LOC):
                    p.op("vector",
                         lambda e, pt=pt, b=b: e.tensor_tensor_scan(
                             Pb[:, b, 1:],
                             par["dd"][:, pt:pt + 1].broadcast_to((128, Tc)),
                             vv1[:, b, :], cP[:, pt, b, :], ALU.mult, ALU.add),
                         reads=(vv1, cP, par["dd"]), writes=(Pb,),
                         tag=f"a1c{ch}scP{pt}_{b}")
                    p.op("vector",
                         lambda e, pt=pt, b=b: e.tensor_tensor_scan(
                             Qb[:, b, 1:],
                             par["dd"][:, pt:pt + 1].broadcast_to((128, Tc)),
                             EK1[:, b, :], cQ[:, pt, b, :], ALU.mult, ALU.add),
                         reads=(EK1, cQ, par["dd"]), writes=(Qb,),
                         tag=f"a1c{ch}scQ{pt}_{b}")
                p.op("vector",
                     lambda e, pt=pt: e.tensor_copy(Pb[:, :, 0:1],
                                                    cP[:, pt, :, :]),
                     reads=(cP,), writes=(Pb,), tag=f"a1c{ch}ph{pt}")
                p.op("vector",
                     lambda e, pt=pt: e.tensor_copy(Qb[:, :, 0:1],
                                                    cQ[:, pt, :, :]),
                     reads=(cQ,), writes=(Qb,), tag=f"a1c{ch}qh{pt}")
                p.op("vector",
                     lambda e, pt=pt: e.tensor_copy(cP[:, pt, :, :],
                                                    Pb[:, :, Tc:Tc + 1]),
                     reads=(Pb,), writes=(cP,), tag=f"a1c{ch}cp{pt}")
                p.op("vector",
                     lambda e, pt=pt: e.tensor_copy(cQ[:, pt, :, :],
                                                    Qb[:, :, Tc:Tc + 1]),
                     reads=(Qb,), writes=(cQ,), tag=f"a1c{ch}cq{pt}")
                p.op("vector",
                     lambda e, pt=pt: e.scalar_tensor_tensor(
                         vv1.ap, vv1.ap, par["eu"][:, pt:pt + 1],
                         Pb[:, :, 0:Tc], ALU.mult, ALU.add),
                     reads=(vv1, Pb, par["eu"]), writes=(vv1,),
                     tag=f"a1c{ch}num{pt}")
                p.op("vector",
                     lambda e, pt=pt: e.scalar_tensor_tensor(
                         EK1.ap, EK1.ap, par["eu"][:, pt:pt + 1],
                         Qb[:, :, 0:Tc], ALU.mult, ALU.add),
                     reads=(EK1, Qb, par["eu"]), writes=(EK1,),
                     tag=f"a1c{ch}den{pt}")
                p.op("vector",
                     lambda e: e.reciprocal(EK1.ap, EK1.ap),
                     reads=(EK1,), writes=(EK1,), tag=f"a1c{ch}rd{pt}")
                p.op("vector",
                     lambda e: e.tensor_tensor(vv1.ap, vv1.ap, EK1.ap, ALU.mult),
                     reads=(vv1, EK1), writes=(vv1,), tag=f"a1c{ch}y{pt}")
                p.op("vector",
                     lambda e, pt=pt: e.tensor_tensor(
                         ysr[:, pt, :, :], vv1.ap, sig1.ap, ALU.mult),
                     reads=(vv1, sig1), writes=(ysr,), tag=f"a1c{ch}ysr{pt}")
                if DBG == "wkv":
                    p.op("vector",
                         lambda e, pt=pt: e.tensor_copy(
                             fl(y, pt), vv1.h[:].rearrange("p b t -> p (b t)")),
                         reads=(vv1,), writes=(y,), tag="dbgw")
            if DBG == "kvr":
                _store_dbg(ch, t0, y)
                continue
            if DBG == "wkv":
                _store_dbg(ch, t0, y)
                continue
            for mt in range(NPT):
                po = psX[mt % 2]
                for kt in range(NPT):
                    p.op("tensor",
                         lambda e, mt=mt, kt=kt, po=po: e.matmul(
                             po.ap, wts["Wo"][:, kt, mt * 128:(mt + 1) * 128],
                             fl(ysr, kt), start=(kt == 0), stop=(kt == NPT - 1)),
                         reads=(ysr, wts["Wo"]), writes=(po,),
                         tag=f"a1c{ch}mmWo{mt}_{kt}")
                for b in range(B_LOC):
                    p.op("scalar",
                         lambda e, mt=mt, b=b, po=po: e.copy(
                             z3_t[:, :, mt * 4 + b],
                             po.h[:, b * Tc:(b + 1) * Tc]),
                         reads=(po,), writes=(z3h[0], z3h[1]),
                         tag=f"a1c{ch}zt{mt}_{b}")
            if DBG == "tm":
                for pt in range(NPT):
                    for b in range(B_LOC):
                        p.op("vector",
                             lambda e, pt=pt, b=b: e.tensor_copy(
                                 y[:, pt, b, :], z3_t[:, :, pt * 4 + b]),
                             reads=(z3h[0], z3h[1]), writes=(y,), tag="dbgt")
                _store_dbg(ch, t0, y)
                continue
            for t in range(Tc):
                for hf in range(2):
                    sl = slice(hf * 16, (hf + 1) * 16)
                    p.op("vector",
                         lambda e, t=t, sl=sl: e.scalar_tensor_tensor(
                             z3_t[:, t, sl], lifc_t[:, sl], 0.5, z3_t[:, t, sl],
                             ALU.mult, ALU.add),
                         reads=(lifch[hf], z3h[hf]), writes=(z3h[hf],),
                         tag=f"a1c{ch}lz{t}_{hf}")
                for hf in range(2):
                    sl = slice(hf * 16, (hf + 1) * 16)
                    p.op("vector",
                         lambda e, t=t, sl=sl: e.scalar_tensor_tensor(
                             lifc_t[:, sl], z3_t[:, t, sl], 1.0, z3_t[:, t, sl],
                             ALU.is_lt, ALU.mult),
                         reads=(z3h[hf],), writes=(lifch[hf],),
                         tag=f"a1c{ch}lc{t}_{hf}")
            for pt in range(NPT):
                for b in range(B_LOC):
                    p.op("vector",
                         lambda e, pt=pt, b=b: e.scalar_tensor_tensor(
                             y[:, pt, b, :], z3_t[:, :, pt * 4 + b], 1.0,
                             y[:, pt, b, :], ALU.is_ge, ALU.add),
                         reads=(z3h[0], z3h[1], y), writes=(y,),
                         tag=f"a1c{ch}sp{pt}_{b}")
            for pt in range(NPT):
                p.op("scalar",
                     lambda e, pt=pt, t0=t0: e.dma_start(
                         dram[out_dram_name][:, pt * 128:(pt + 1) * 128,
                                             t0:t0 + Tc].transpose([1, 0, 2]),
                         y[:, pt, :, :]),
                     reads=(y,), writes=(dbuf[out_dram_name],), is_dma=True,
                     is_store=(out_dram_name == "out"), tag=f"a1c{ch}sty{pt}")

    def emit_a2a():
        Tc = 128
        F = B_LOC * Tc
        NCH = T // Tc
        fWk_b = sb("a2_fWk", [128, NPT, HID], BF16)
        wst2 = sb("a2_wst2", [128, HID // 4])
        wstage = Buf("a2_wstage", wst2.h)
        for kt in range(NPT):
            for hf in range(4):
                p.op("sync",
                     lambda e, kt=kt, hf=hf: e.dma_start(
                         wst2.ap,
                         dram["fWk"][kt * 128:(kt + 1) * 128,
                                     hf * (HID // 4):(hf + 1) * (HID // 4)]),
                     reads=(dbuf["fWk"],), writes=(wst2,), is_dma=True,
                     tag=f"a2ldk{kt}_{hf}")
                p.op("vector",
                     lambda e, kt=kt, hf=hf: e.tensor_copy(
                         fWk_b[:, kt, hf * (HID // 4):(hf + 1) * (HID // 4)],
                         wst2.ap),
                     reads=(wst2,), writes=(fWk_b,), tag=f"a2ck{kt}_{hf}")
        fWv_b = sb("a2_fWv", [128, NHT, C], BF16)
        for kt in range(NHT):
            p.op("sync",
                 lambda e, kt=kt: e.dma_start(
                     wst2[:, 0:C], dram["fWv"][kt * 128:(kt + 1) * 128, :]),
                 reads=(dbuf["fWv"],), writes=(wstage, wst2), is_dma=True,
                 tag=f"a2ldv{kt}")
            p.op("vector",
                 lambda e, kt=kt: e.tensor_scalar(
                     fWv_b[:, kt, :], wst2[:, 0:C], 0.5, None, ALU.mult),
                 reads=(wstage, wst2), writes=(fWv_b,), tag=f"a2cv{kt}")

        y2t = sb("a2_y2t", [128, NPT, B_LOC, Tc])
        h2tail = sb("a2_h2tail", [128, NPT, B_LOC, 1], BF16)
        xk2 = sb("a2_xk2", [128, NPT, B_LOC, Tc], BF16)
        kk = sb("a2_kk", [128, NHT, B_LOC, Tc], BF16)
        delta = Buf("a2_delta", kk.h)
        h2 = Buf("a2_h2", kk.h)
        scr = {
            "xb0": sb("a2_xb0", [128, F], BF16),
            "xb1": sb("a2_xb1", [128, F], BF16),
            "sq0": sb("a2_sq0", [128, F], BF16),
            "sq1": sb("a2_sq1", [128, F], BF16),
            "m_b": sb("a2_mb", [128, F]),
            "r_b": sb("a2_rb", [128, F]),
            "v1": sb("a2_v1", [128, F]),
        }
        psA = psb("a2_psA", [128, F])
        psB = psb("a2_psB", [128, F])
        scr["psA"], scr["psB"] = psA, psB
        psK = [psb(f"a2_psK{i}", [128, F]) for i in range(2)]
        psC = [psb(f"a2_psC{i}", [128, F]) for i in range(2)]
        rstg = [sb("a2_rstg", [128, F])] * 2

        fill(h2tail, 0.0)

        def fl(buf, pt):
            return buf.h[:, pt, :, :].rearrange("p b t -> p (b t)")

        def h2s(pt):
            return kk.h[:, NPT + pt, :, :]

        def h2f(pt):
            return kk.h[:, NPT + pt, :, :].rearrange("p b t -> p (b t)")

        for ch in range(NCH):
            t0 = ch * Tc
            for pt in range(NPT):
                p.op("sync",
                     lambda e, pt=pt, t0=t0: e.dma_start(
                         y2t[:, pt, :, :],
                         dram["y2"][:, pt * 128:(pt + 1) * 128, t0:t0 + Tc]
                         .transpose([1, 0, 2])),
                     reads=(dbuf["y2"],), writes=(y2t,), is_dma=True,
                     tag=f"a2ldy{ch}_{pt}")
            emit_ln_stats(y2t, lambda pt: fl(y2t, pt), F, scr,
                          tag=f"a2c{ch}ln_")
            for pt in range(NPT):
                p.op("vector",
                     lambda e, pt=pt: e.tensor_tensor(
                         h2f(pt), fl(y2t, pt), scr["m_b"].ap, ALU.subtract),
                     reads=(y2t, scr["m_b"]), writes=(h2, kk), tag=f"a2c{ch}sb{pt}")
            for pt in range(NPT):
                p.op("vector",
                     lambda e, pt=pt: e.tensor_tensor(
                         h2f(pt), h2f(pt), scr["r_b"].ap, ALU.mult),
                     reads=(h2, scr["r_b"]), writes=(h2,), tag=f"a2c{ch}nr{pt}")
            for pt in range(NPT):
                p.op("scalar",
                     lambda e, pt=pt: e.activation(
                         h2f(pt), h2f(pt), AF.Identity,
                         bias=par["ln2_b"][:, pt:pt + 1],
                         scale=par["ln2_s"][:, pt:pt + 1]),
                     reads=(h2, par["ln2_s"], par["ln2_b"]), writes=(h2,),
                     tag=f"a2c{ch}af{pt}")
            for pt in range(NPT):
                p.op("vector",
                     lambda e, pt=pt: e.tensor_copy(fl(y2t, pt), h2f(pt)),
                     reads=(h2,), writes=(y2t,), tag=f"a2c{ch}cf{pt}")
            for pt in range(NPT):
                p.op("scalar",
                     lambda e, pt=pt, t0=t0: e.dma_start(
                         dram["h2d"][:, pt * 128:(pt + 1) * 128, t0:t0 + Tc]
                         .transpose([1, 0, 2]), y2t[:, pt, :, :]),
                     reads=(y2t,), writes=(dbuf["h2d"],), is_dma=True,
                     tag=f"a2c{ch}sh{pt}")
            for pt in range(NPT):
                p.op("vector",
                     lambda e, pt=pt: e.tensor_tensor(
                         delta.h[:, pt, :, 1:], h2s(pt)[:, :, 1:],
                         h2s(pt)[:, :, 0:Tc - 1], ALU.subtract),
                     reads=(h2,), writes=(delta, kk), tag=f"a2c{ch}dl{pt}")
                p.op("vector",
                     lambda e, pt=pt: e.tensor_tensor(
                         delta.h[:, pt, :, 0:1], h2s(pt)[:, :, 0:1],
                         h2tail[:, pt, :, :], ALU.subtract),
                     reads=(h2, h2tail), writes=(delta, kk),
                     tag=f"a2c{ch}dlb{pt}")
            for pt in range(NPT):
                p.op("vector",
                     lambda e, pt=pt: e.scalar_tensor_tensor(
                         xk2[:, pt, :, 1:], delta.h[:, pt, :, 1:],
                         par["f_tmk"][:, pt:pt + 1], h2s(pt)[:, :, 0:Tc - 1],
                         ALU.mult, ALU.add),
                     reads=(delta, h2, par["f_tmk"]), writes=(xk2,),
                     tag=f"a2c{ch}mx{pt}")
                p.op("vector",
                     lambda e, pt=pt: e.scalar_tensor_tensor(
                         xk2[:, pt, :, 0:1], delta.h[:, pt, :, 0:1],
                         par["f_tmk"][:, pt:pt + 1], h2tail[:, pt, :, :],
                         ALU.mult, ALU.add),
                     reads=(delta, h2tail, par["f_tmk"]), writes=(xk2,),
                     tag=f"a2c{ch}mxb{pt}")
            p.op("vector", lambda e: e.tensor_copy(h2tail.ap,
                                                   kk.h[:, NPT:2 * NPT, :, Tc - 1:Tc]),
                 reads=(h2,), writes=(h2tail,), tag=f"a2c{ch}tl")
            for mt in range(NHT):
                pk = psK[mt % 2]
                for kt in range(NPT):
                    p.op("tensor",
                         lambda e, mt=mt, kt=kt, pk=pk: e.matmul(
                             pk.ap, fWk_b[:, kt, mt * 128:(mt + 1) * 128],
                             fl(xk2, kt), start=(kt == 0), stop=(kt == NPT - 1)),
                         reads=(xk2, fWk_b), writes=(pk,),
                         tag=f"a2c{ch}mmk{mt}_{kt}")
                rs = rstg[mt % 2]
                p.op("scalar",
                     lambda e, mt=mt, pk=pk, rs=rs: e.activation(
                         rs.ap, pk.ap, AF.Relu),
                     reads=(pk,), writes=(rs,), tag=f"a2c{ch}rl{mt}")
                p.op("vector",
                     lambda e, mt=mt, rs=rs: e.tensor_tensor(
                         fl(kk, mt), rs.ap, rs.ap, ALU.mult),
                     reads=(rs,), writes=(kk,), tag=f"a2c{ch}rq{mt}")
            for mt in range(NPT):
                pc_ = psC[mt % 2]
                for kt in range(NHT):
                    p.op("tensor",
                         lambda e, mt=mt, kt=kt, pc_=pc_: e.matmul(
                             pc_.ap, fWv_b[:, kt, mt * 128:(mt + 1) * 128],
                             fl(kk, kt), start=(kt == 0), stop=(kt == NHT - 1)),
                         reads=(kk, fWv_b), writes=(pc_,),
                         tag=f"a2c{ch}mmv{mt}_{kt}")
                p.op("scalar",
                     lambda e, mt=mt, pc_=pc_: e.copy(fl(y2t, mt), pc_.ap),
                     reads=(pc_,), writes=(y2t,), tag=f"a2c{ch}kve{mt}")
            for pt in range(NPT):
                p.op("scalar",
                     lambda e, pt=pt, t0=t0: e.dma_start(
                         dram["kv"][:, pt * 128:(pt + 1) * 128, t0:t0 + Tc]
                         .transpose([1, 0, 2]), y2t[:, pt, :, :]),
                     reads=(y2t,), writes=(dbuf["kv"],), is_dma=True,
                     tag=f"a2c{ch}skv{pt}")

    def emit_a2b(out_name):
        Tc = 128
        F = B_LOC * Tc
        NCH = T // Tc
        wstage = sb("b2_wstage", [128, C])
        fWr_b = sb("b2_fWr", [128, NPT, C], BF16)
        for kt in range(NPT):
            p.op("sync",
                 lambda e, kt=kt: e.dma_start(
                     wstage.ap, dram["fWr"][kt * 128:(kt + 1) * 128, :]),
                 reads=(dbuf["fWr"],), writes=(wstage,), is_dma=True,
                 tag=f"b2ldr{kt}")
            p.op("vector",
                 lambda e, kt=kt: e.tensor_copy(fWr_b[:, kt, :], wstage.ap),
                 reads=(wstage,), writes=(fWr_b,), tag=f"b2cr{kt}")

        y2t = sb("b2_y2t", [128, NPT, B_LOC, Tc])
        xts = sb("b2_xts", [128, NPT, B_LOC, Tc])
        kvt = sb("b2_kvt", [128, NPT, B_LOC, Tc])
        h2 = sb("b2_h2", [128, NPT, B_LOC, Tc])
        h2tail = sb("b2_h2tail", [128, NPT, B_LOC, 1])
        delta = sb("b2_delta", [128, NPT, B_LOC, Tc], BF16)
        xr2 = sb("b2_xr2", [128, NPT, B_LOC, Tc], BF16)
        sig1 = sb("b2_sig1", [128, B_LOC, Tc])
        z3_t = sb("b2_z3", [128, Tc, 32]).h
        z3h = [Buf(f"b2_z3h{i}", z3_t) for i in range(2)]
        lifc_t = sb("b2_lifc", [128, 32]).h
        lifch = [Buf(f"b2_lifc{i}", lifc_t) for i in range(2)]
        psR = psb("b2_psR", [128, F])

        fill(h2tail, 0.0)
        fill(lifch[0], 0.0)

        def fl(buf, pt):
            return buf.h[:, pt, :, :].rearrange("p b t -> p (b t)")

        for ch in range(NCH):
            t0 = ch * Tc
            for pt in range(NPT):
                for dst, dn in ((y2t, "y2"), (xts, "x"), (kvt, "kv"),
                                (h2, "h2d")):
                    p.op("sync",
                         lambda e, pt=pt, t0=t0, dst=dst, dn=dn: e.dma_start(
                             dst[:, pt, :, :],
                             dram[dn][:, pt * 128:(pt + 1) * 128, t0:t0 + Tc]
                             .transpose([1, 0, 2])),
                         reads=(dbuf[dn],), writes=(dst,), is_dma=True,
                         tag=f"b2ld{dn}{ch}_{pt}")
            for pt in range(NPT):
                p.op("vector",
                     lambda e, pt=pt: e.tensor_tensor(
                         delta[:, pt, :, 1:], h2[:, pt, :, 1:],
                         h2[:, pt, :, 0:Tc - 1], ALU.subtract),
                     reads=(h2,), writes=(delta,), tag=f"b2c{ch}dl{pt}")
                p.op("vector",
                     lambda e, pt=pt: e.tensor_tensor(
                         delta[:, pt, :, 0:1], h2[:, pt, :, 0:1],
                         h2tail[:, pt, :, :], ALU.subtract),
                     reads=(h2, h2tail), writes=(delta,), tag=f"b2c{ch}dlb{pt}")
            for pt in range(NPT):
                p.op("vector",
                     lambda e, pt=pt: e.scalar_tensor_tensor(
                         xr2[:, pt, :, 1:], delta[:, pt, :, 1:],
                         par["f_tmr"][:, pt:pt + 1], h2[:, pt, :, 0:Tc - 1],
                         ALU.mult, ALU.add),
                     reads=(delta, h2, par["f_tmr"]), writes=(xr2,),
                     tag=f"b2c{ch}mr{pt}")
                p.op("vector",
                     lambda e, pt=pt: e.scalar_tensor_tensor(
                         xr2[:, pt, :, 0:1], delta[:, pt, :, 0:1],
                         par["f_tmr"][:, pt:pt + 1], h2tail[:, pt, :, :],
                         ALU.mult, ALU.add),
                     reads=(delta, h2tail, par["f_tmr"]), writes=(xr2,),
                     tag=f"b2c{ch}mrb{pt}")
            p.op("vector", lambda e: e.tensor_copy(h2tail.ap,
                                                   h2[:, :, :, Tc - 1:Tc]),
                 reads=(h2,), writes=(h2tail,), tag=f"b2c{ch}tl")
            for mt in range(NPT):
                for kt in range(NPT):
                    p.op("tensor",
                         lambda e, mt=mt, kt=kt: e.matmul(
                             psR.ap, fWr_b[:, kt, mt * 128:(mt + 1) * 128],
                             fl(xr2, kt), start=(kt == 0), stop=(kt == NPT - 1)),
                         reads=(xr2, fWr_b), writes=(psR,),
                         tag=f"b2c{ch}mmr{mt}_{kt}")
                p.op("scalar",
                     lambda e, mt=mt: e.activation(
                         sig1.h[:].rearrange("p b t -> p (b t)"), psR.ap,
                         AF.Exp, scale=-1.0),
                     reads=(psR,), writes=(sig1,), tag=f"b2c{ch}sg{mt}")
                p.op("vector",
                     lambda e: e.tensor_scalar(sig1.ap, sig1.ap, 1.0, None,
                                               ALU.add),
                     reads=(sig1,), writes=(sig1,), tag=f"b2c{ch}s1{mt}")
                p.op("vector",
                     lambda e: e.reciprocal(sig1.ap, sig1.ap),
                     reads=(sig1,), writes=(sig1,), tag=f"b2c{ch}sr{mt}")
                for b in range(B_LOC):
                    p.op("vector",
                         lambda e, mt=mt, b=b: e.tensor_tensor(
                             z3_t[:, :, mt * 4 + b], kvt[:, mt, b, :],
                             sig1[:, b, :], ALU.mult),
                         reads=(kvt, sig1), writes=(z3h[0], z3h[1]),
                         tag=f"b2c{ch}cm{mt}_{b}")
            for t in range(Tc):
                for hf in range(2):
                    sl = slice(hf * 16, (hf + 1) * 16)
                    p.op("vector",
                         lambda e, t=t, sl=sl: e.scalar_tensor_tensor(
                             z3_t[:, t, sl], lifc_t[:, sl], 0.5, z3_t[:, t, sl],
                             ALU.mult, ALU.add),
                         reads=(lifch[hf], z3h[hf]), writes=(z3h[hf],),
                         tag=f"b2c{ch}lz{t}_{hf}")
                for hf in range(2):
                    sl = slice(hf * 16, (hf + 1) * 16)
                    p.op("vector",
                         lambda e, t=t, sl=sl: e.scalar_tensor_tensor(
                             lifc_t[:, sl], z3_t[:, t, sl], 1.0, z3_t[:, t, sl],
                             ALU.is_lt, ALU.mult),
                         reads=(z3h[hf],), writes=(lifch[hf],),
                         tag=f"b2c{ch}lc{t}_{hf}")
            for pt in range(NPT):
                for b in range(B_LOC):
                    p.op("vector",
                         lambda e, pt=pt, b=b: e.scalar_tensor_tensor(
                             y2t[:, pt, b, :], z3_t[:, :, pt * 4 + b], 1.0,
                             y2t[:, pt, b, :], ALU.is_ge, ALU.add),
                         reads=(z3h[0], z3h[1], y2t), writes=(y2t,),
                         tag=f"b2c{ch}sp{pt}_{b}")
            for pt in range(NPT):
                p.op("vector",
                     lambda e, pt=pt: e.tensor_tensor(
                         fl(y2t, pt), fl(y2t, pt), fl(xts, pt), ALU.add),
                     reads=(y2t, xts), writes=(y2t,), tag=f"b2c{ch}x1{pt}")
            for pt in range(NPT):
                p.op("scalar",
                     lambda e, pt=pt, t0=t0: e.dma_start(
                         dram[out_name][:, pt * 128:(pt + 1) * 128, t0:t0 + Tc]
                         .transpose([1, 0, 2]), y2t[:, pt, :, :]),
                     reads=(y2t,), writes=(dbuf[out_name],), is_dma=True,
                     is_store=(out_name == "out"), tag=f"b2c{ch}st{pt}")

    def bcast_scalar(src_row, src_buf, dst_col, slot, tagp):
        p.op("scalar", lambda e: e.dma_start(dram["sca"][slot:slot + 1, :],
                                             src_row),
             reads=(src_buf,), writes=(dbuf["sca"],), is_dma=True,
             tag=tagp + "ssc")
        p.op("sync", lambda e: e.dma_start(
            dst_col.ap, dram["sca"][slot:slot + 1, :].broadcast_to((128, 1))),
            reads=(dbuf["sca"],), writes=(dst_col,), is_dma=True,
            tag=tagp + "lsc")

    def pmax_rows(fold_bufs, grp, rowseg, psT, colT, ident, tagp, F, red_ops):
        ncols = 0
        for buf, op in red_ops:
            for g in range(F // 128):
                p.op("tensor",
                     lambda e, buf=buf, g=g: e.transpose(
                         psT.ap, buf.h[:, g * 128:(g + 1) * 128], ident.ap),
                     reads=(buf, ident), writes=(psT,), tag=f"{tagp}tr{ncols}")
                p.op("vector",
                     lambda e, g=g, ncols=ncols, op=op: e.tensor_reduce(
                         colT[:, ncols:ncols + 1], psT.ap, AX.X,
                         ALU.max if op == "max" else ALU.min),
                     reads=(psT,), writes=(colT,), tag=f"{tagp}rd{ncols}")
                ncols += 1

    def emit_b1():
        Tc = 128
        F = B_LOC * Tc
        NCH = T // Tc
        NG = F // 128
        SRC = B1_SRC
        wst = sb("b1_wst", [128, HID])
        w1q = sb("b1_w1q", [128, NPT, HID], BF16)
        acc = sb("b1_acc", [128, 1])
        red = sb("b1_red", [128, 1])
        ws1c = sb("b1_ws1c", [128, 1])
        dqw1c = sb("b1_dqw1c", [128, 1])
        one11 = sb("b1_one11", [128, 2])
        psT = psb("b1_psT", [128, 128])
        if not B1_WPREP:
            fill(w1q, 0.0); fill(ws1c, 1.0); fill(dqw1c, 1.0)
        wprep = B1_WPREP
        fill(acc, 0.0)
        for kt in range(NPT if wprep else 0):
            p.op("sync", lambda e, kt=kt: e.dma_start(
                wst.ap, dram["w1T"][kt * 128:(kt + 1) * 128, :]),
                reads=(dbuf["w1T"],), writes=(wst,), is_dma=True,
                tag=f"b1lw{kt}")
            p.op("vector", lambda e: e.reduce_sum(red.ap, wst.ap, AX.X,
                                                  apply_absolute_value=True),
                 reads=(wst,), writes=(red,), tag=f"b1rs{kt}")
            p.op("vector", lambda e: e.tensor_tensor(acc.ap, acc.ap, red.ap,
                                                     ALU.add),
                 reads=(acc, red), writes=(acc,), tag=f"b1ac{kt}")
        if not wprep:
            _b1_skip = True
        p.op("tensor", lambda e: e.transpose(psT[0:1, 0:128], acc.ap,
                                             ident_f.ap),
             reads=(acc, ident_f), writes=(psT,), tag="b1trA") if wprep else None
        p.op("vector", lambda e: e.reduce_sum(one11[0:1, 0:1], psT[0:1, 0:128],
                                              AX.X),
             reads=(psT,), writes=(one11,), tag="b1sumA") if wprep else None
        p.op("vector", lambda e: e.tensor_scalar(
            one11[0:1, 0:1], one11[0:1, 0:1], 1.0 / (C * HID), None, ALU.mult),
            reads=(one11,), writes=(one11,), tag="b1mean") if wprep else None
        p.op("vector", lambda e: e.reciprocal(one11[0:1, 1:2], one11[0:1, 0:1]),
             reads=(one11,), writes=(one11,), tag="b1ws") if wprep else None
        p.op("vector", lambda e: e.tensor_scalar(
            one11[0:1, 0:1], one11[0:1, 0:1], 1.0 / 127.0, None, ALU.mult),
            reads=(one11,), writes=(one11,), tag="b1dq") if wprep else None
        if wprep:
            bcast_scalar(one11[0:1, 1:2], one11, ws1c, 0, "b1w")
            bcast_scalar(one11[0:1, 0:1], one11, dqw1c, 1, "b1d")
        for kt in range(NPT if wprep else 0):
            p.op("sync", lambda e, kt=kt: e.dma_start(
                wst.ap, dram["w1T"][kt * 128:(kt + 1) * 128, :]),
                reads=(dbuf["w1T"],), writes=(wst,), is_dma=True,
                tag=f"b1lq{kt}")
            p.op("vector", lambda e: e.tensor_scalar(
                wst.ap, wst.ap, ws1c.ap, None, ALU.mult),
                reads=(wst, ws1c), writes=(wst,), tag=f"b1q1_{kt}")
            p.op("vector", lambda e: e.tensor_scalar(
                wst.ap, wst.ap, magic.ap, magic.ap, ALU.add, ALU.subtract),
                reads=(wst, magic), writes=(wst,), tag=f"b1q2_{kt}")
            p.op("vector", lambda e, kt=kt: e.tensor_scalar(
                w1q[:, kt, :], wst.ap, 1.0, -1.0, ALU.min, ALU.max),
                reads=(wst,), writes=(w1q,), tag=f"b1q3_{kt}")

        x1t = sb("b1_x1t", [128, 2, NPT, B_LOC, Tc])
        q1 = sb("b1_q1", [128, NPT, B_LOC, Tc], BF16)
        hs = sb("b1_hs", [128, NHT, B_LOC, Tc], BF16)
        w0d = [sb(f"b1_w0{i}", [128, F]) for i in range(2)]
        fmxd = [sb(f"b1_fmx{i}", [128, F]) for i in range(2)]
        fmnd = [sb(f"b1_fmn{i}", [128, F]) for i in range(2)]
        colTd = [sb(f"b1_colT{i}", [128, 2 * NG]) for i in range(2)]
        colUd = [sb(f"b1_colU{i}", [B_LOC, 128]) for i in range(2)]
        slot3 = [sb(f"b1_slot{i}", [128, B_LOC, Tc + 3]) for i in range(2)]
        ctmp = [sb(f"b1_ctmp{i}", [128, B_LOC, Tc]) for i in range(2)]
        tails = sb("b1_tails", [128, NHT, B_LOC, 2])
        habsd = [sb(f"b1_habs{i}", [128, F]) for i in range(2)]
        hsE = sb("b1_hsE", [128, NHT, B_LOC, 1], BF16)
        hE32 = sb("b1_hE32", [128, NHT, B_LOC, 1])
        RS_bd = [sb(f"b1_RSb{i}", [128, F]) for i in range(2)]
        DQ_bd = [sb(f"b1_DQb{i}", [128, F]) for i in range(2)]
        psA = psb("b1_psA", [128, F])
        psB = psb("b1_psB", [128, F])
        psBC = psb("b1_psBC", [128, F])
        xbs = [sb("b1_xb0", [128, F], BF16), sb("b1_xb1", [128, F], BF16)]
        sqs = [sb("b1_sq0", [128, F], BF16), sb("b1_sq1", [128, F], BF16)]
        scrd = []
        for i in range(2):
            s_ = {
                "xb0": xbs[0], "xb1": xbs[1], "sq0": sqs[0], "sq1": sqs[1],
                "m_b": sb(f"b1_mb{i}", [128, F]),
                "r_b": sb(f"b1_rb{i}", [128, F]),
                "v1": sb(f"b1_v1{i}", [128, F]),
                "psA": psA, "psB": psB,
            }
            scrd.append(s_)
        psH = [psb(f"b1_psH{i}", [128, F]) for i in range(2)]

        fill(tails, 0.0)

        def fl(buf, pt):
            return buf.h[:, pt, :, :].rearrange("p b t -> p (b t)")


        def b1_chunk(ch, scr, w0, fmx, fmn, colT, colU, habs, RS_b, DQ_b):
            t0 = ch * Tc
            cur, nxt = ch % 2, (ch + 1) % 2
            if ch == 0:
                for pt in range(NPT):
                    p.op("sync", lambda e, pt=pt, t0=t0, cur=cur: e.dma_start(
                        x1t[:, cur, pt, :, :],
                        dram[SRC][:, pt * 128:(pt + 1) * 128, t0:t0 + Tc]
                        .transpose([1, 0, 2])),
                        reads=(dbuf[SRC],), writes=(x1t,), is_dma=True,
                        tag=f"b1lx{ch}_{pt}")
            if ch + 1 < NCH:
                for pt in range(NPT):
                    p.op("sync", lambda e, pt=pt, t1=t0 + Tc, nxt=nxt:
                         e.dma_start(
                             x1t[:, nxt, pt, :, :],
                             dram[SRC][:, pt * 128:(pt + 1) * 128, t1:t1 + Tc]
                             .transpose([1, 0, 2])),
                         reads=(dbuf[SRC],), writes=(x1t,), is_dma=True,
                         tag=f"b1lx{ch + 1}_{pt}")
            xv_ = lambda pt, cur=cur: x1t.h[:, cur, pt, :, :] \
                .rearrange("p b t -> p (b t)")
            if DBG == "x1chk":
                for pt in range(NPT):
                    p.op("scalar", lambda e, pt=pt, t0=t0, cur=cur: e.dma_start(
                        dram["out"][:, pt * 128:(pt + 1) * 128, t0:t0 + Tc]
                        .transpose([1, 0, 2]), x1t[:, cur, pt, :, :]),
                        reads=(x1t,), writes=(dbuf["out"],), is_dma=True,
                        is_store=True, tag="dbgx1s")
                return
            emit_ln_stats(x1t, xv_, F, scr, tag=f"b1c{ch}ln_")
            if DBG == "mchk":
                for j, bb in ((0, scr["m_b"]), (1, scr["r_b"]), (2, scr["v1"])):
                    p.op("scalar", lambda e, j=j, bb=bb, t0=t0: e.dma_start(
                        dram["out"][:, j * 128:(j + 1) * 128, t0:t0 + Tc]
                        .transpose([1, 0, 2]),
                        bb.h[:].rearrange("p (b t) -> p b t", b=B_LOC, t=Tc)),
                        reads=(bb,), writes=(dbuf["out"],), is_dma=True,
                        is_store=True, tag=f"dbgm{j}")
                return
            for pt in range(NPT):
                if pt == 0:
                    p.op("vector", lambda e, xv_=xv_: e.tensor_copy(
                        fmx.ap, xv_(0)),
                        reads=(x1t,), writes=(fmx,), tag=f"b1c{ch}fx0")
                    p.op("vector", lambda e, xv_=xv_: e.tensor_copy(
                        fmn.ap, xv_(0)),
                        reads=(x1t,), writes=(fmn,), tag=f"b1c{ch}fn0")
                else:
                    p.op("vector", lambda e, pt=pt, xv_=xv_: e.tensor_tensor(
                        fmx.ap, fmx.ap, xv_(pt), ALU.max),
                        reads=(fmx, x1t), writes=(fmx,), tag=f"b1c{ch}fx{pt}")
                    p.op("vector", lambda e, pt=pt, xv_=xv_: e.tensor_tensor(
                        fmn.ap, fmn.ap, xv_(pt), ALU.min),
                        reads=(fmn, x1t), writes=(fmn,), tag=f"b1c{ch}fn{pt}")
            pmax_rows(None, NG, None, psT, colT, ident_f, f"b1c{ch}am",
                      F, [(fmx, "max"), (fmn, "min")])
            p.op("scalar", lambda e, ch=ch: e.dma_start(
                dram["rows"][0 + 0:1, ch * F:(ch + 1) * F]
                .rearrange("o (g q) -> (o q) g", g=NG, q=128),
                colT[:, 0:NG], allow_slow_non_contiguous=True),
                reads=(colT,), writes=(dbuf["rows%d" % (ch % 2)],), is_dma=True,
                tag=f"b1c{ch}smx")
            p.op("scalar", lambda e, ch=ch: e.dma_start(
                dram["rows"][1:2, ch * F:(ch + 1) * F]
                .rearrange("o (g q) -> (o q) g", g=NG, q=128),
                colT[:, NG:2 * NG], allow_slow_non_contiguous=True),
                reads=(colT,), writes=(dbuf["rows%d" % (ch % 2)],), is_dma=True,
                tag=f"b1c{ch}smn")
            p.op("sync", lambda e, ch=ch: e.dma_start(
                w0[0:1, 0:F], dram["rows"][0 + 0:1, ch * F:(ch + 1) * F]),
                reads=(dbuf["rows%d" % (ch % 2)],), writes=(w0,), is_dma=True,
                tag=f"b1c{ch}lmx")
            p.op("sync", lambda e, ch=ch: e.dma_start(
                habs[0:1, 0:F], dram["rows"][1:2, ch * F:(ch + 1) * F]),
                reads=(dbuf["rows%d" % (ch % 2)],), writes=(habs,), is_dma=True,
                tag=f"b1c{ch}lmn")
            p.op("vector", lambda e: e.tensor_tensor(
                w0[0:1, 0:F], w0[0:1, 0:F], scr["m_b"][0:1, 0:F], ALU.subtract),
                reads=(w0, scr["m_b"]), writes=(w0,), tag=f"b1c{ch}r1")
            p.op("vector", lambda e: e.tensor_tensor(
                habs[0:1, 0:F], scr["m_b"][0:1, 0:F], habs[0:1, 0:F],
                ALU.subtract),
                reads=(habs, scr["m_b"]), writes=(habs,), tag=f"b1c{ch}r2")
            p.op("vector", lambda e: e.tensor_tensor(
                w0[0:1, 0:F], w0[0:1, 0:F], habs[0:1, 0:F], ALU.max),
                reads=(w0, habs), writes=(w0,), tag=f"b1c{ch}r3")
            p.op("vector", lambda e: e.tensor_tensor(
                habs[0:1, 0:F], w0[0:1, 0:F], scr["r_b"][0:1, 0:F], ALU.mult),
                reads=(w0, scr["r_b"]), writes=(habs,), tag=f"b1c{ch}r4")
            p.op("vector", lambda e: e.tensor_scalar(
                habs[0:1, 0:F], habs[0:1, 0:F], dqw1c[0:1, 0:1], None,
                ALU.mult),
                reads=(habs, dqw1c), writes=(habs,), tag=f"b1c{ch}r5")
            p.op("vector", lambda e: e.tensor_scalar(
                w0[0:1, 0:F], w0[0:1, 0:F], 1.0 / 127.0, None, ALU.mult),
                reads=(w0,), writes=(w0,), tag=f"b1c{ch}r6")
            p.op("vector", lambda e: e.reciprocal(w0[0:1, 0:F], w0[0:1, 0:F]),
                 reads=(w0,), writes=(w0,), tag=f"b1c{ch}r7")
            p.op("tensor", lambda e, w0=w0: e.matmul(
                psBC[:, 0:F], ones_f.ap, w0[0:1, 0:F], start=True, stop=True),
                reads=(w0, ones_f), writes=(psBC,), tag=f"b1c{ch}brs")
            p.op("scalar", lambda e, RS_b=RS_b: e.activation(
                RS_b.ap, psBC[:, 0:F], AF.Identity),
                reads=(psBC,), writes=(RS_b,), tag=f"b1c{ch}crs")
            p.op("tensor", lambda e, habs=habs: e.matmul(
                psBC[:, 0:F], ones_f.ap, habs[0:1, 0:F], start=True, stop=True),
                reads=(habs, ones_f), writes=(psBC,), tag=f"b1c{ch}bdq")
            p.op("scalar", lambda e, DQ_b=DQ_b: e.activation(
                DQ_b.ap, psBC[:, 0:F], AF.Identity),
                reads=(psBC,), writes=(DQ_b,), tag=f"b1c{ch}cdq")
            for pt in range(NPT):
                p.op("vector", lambda e, pt=pt, xv_=xv_: e.tensor_tensor(
                    w0.ap, xv_(pt), scr["m_b"].ap, ALU.subtract),
                    reads=(x1t, scr["m_b"]), writes=(w0,), tag=f"b1c{ch}qa{pt}")
                p.op("vector", lambda e: e.tensor_tensor(
                    w0.ap, w0.ap, RS_b.ap, ALU.mult),
                    reads=(w0, RS_b), writes=(w0,), tag=f"b1c{ch}qb{pt}")
                p.op("vector", lambda e, pt=pt: e.tensor_scalar(
                    fl(q1, pt), w0.ap, magic.ap, magic.ap, ALU.add,
                    ALU.subtract),
                    reads=(w0, magic), writes=(q1,), tag=f"b1c{ch}qc{pt}")
            if DBG == "rs":
                p.op("scalar", lambda e, t0=t0: e.dma_start(
                    dram["out"][:, 0:128, t0:t0 + Tc].transpose([1, 0, 2]),
                    RS_b.h[:].rearrange("p (b t) -> p b t", b=B_LOC, t=Tc)),
                    reads=(RS_b,), writes=(dbuf["out"],), is_dma=True,
                    is_store=True, tag="dbgrss")
                p.op("scalar", lambda e, t0=t0: e.dma_start(
                    dram["out"][:, 128:256, t0:t0 + Tc].transpose([1, 0, 2]),
                    scr["m_b"].h[:].rearrange("p (b t) -> p b t", b=B_LOC, t=Tc)),
                    reads=(scr["m_b"],), writes=(dbuf["out"],), is_dma=True,
                    is_store=True, tag="dbgmbs")
                p.op("scalar", lambda e, t0=t0: e.dma_start(
                    dram["out"][:, 256:384, t0:t0 + Tc].transpose([1, 0, 2]),
                    DQ_b.h[:].rearrange("p (b t) -> p b t", b=B_LOC, t=Tc)),
                    reads=(DQ_b,), writes=(dbuf["out"],), is_dma=True,
                    is_store=True, tag="dbgdqs")
                p.op("scalar", lambda e, t0=t0: e.dma_start(
                    dram["out"][0:1, 384:512, t0:t0 + 1].transpose([1, 0, 2]),
                    magic.h[:, 0:1].rearrange("p (b t) -> p b t", b=1, t=1),
                    allow_slow_non_contiguous=True),
                    reads=(magic,), writes=(dbuf["out"],), is_dma=True,
                    is_store=True, tag="dbgmgs")
                for j, bb in ((1, ws1c), (2, dqw1c)):
                    p.op("scalar", lambda e, j=j, bb=bb, t0=t0: e.dma_start(
                        dram["out"][0:1, 384:512, t0 + j:t0 + j + 1]
                        .transpose([1, 0, 2]),
                        bb.h[:, 0:1].rearrange("p (b t) -> p b t", b=1, t=1),
                        allow_slow_non_contiguous=True),
                        reads=(bb,), writes=(dbuf["out"],), is_dma=True,
                        is_store=True, tag=f"dbgwc{j}")
                return
            if DBG == "q1":
                for pt in range(NPT):
                    p.op("vector", lambda e, pt=pt: e.tensor_copy(
                        w0.ap, fl(q1, pt)),
                        reads=(q1,), writes=(w0,), tag="dbgq1c")
                    p.op("scalar", lambda e, pt=pt, t0=t0: e.dma_start(
                        dram["out"][:, pt * 128:(pt + 1) * 128, t0:t0 + Tc]
                        .transpose([1, 0, 2]),
                        w0.h[:].rearrange("p (b t) -> p b t", b=B_LOC, t=Tc)),
                        reads=(w0,), writes=(dbuf["out"],), is_dma=True,
                        is_store=True, tag="dbgq1s")
                return
            last = (ch == NCH - 1)
            for mt in range(NHT):
                ph = psH[mt % 2]
                sl3 = slot3[mt % 2]
                ct = ctmp[mt % 2]
                for kt in range(NPT):
                    p.op("tensor", lambda e, mt=mt, kt=kt, ph=ph: e.matmul(
                        ph.ap, w1q[:, kt, mt * 128:(mt + 1) * 128],
                        fl(q1, kt), start=(kt == 0), stop=(kt == NPT - 1)),
                        reads=(q1, w1q), writes=(ph,),
                        tag=f"b1c{ch}mm{mt}_{kt}")
                p.op("vector", lambda e, mt=mt, sl3=sl3: e.tensor_copy(
                    sl3[:, :, 0:2], tails[:, mt, :, :]),
                    reads=(tails,), writes=(sl3,), tag=f"b1c{ch}ti{mt}")
                p.op("vector", lambda e, ph=ph, sl3=sl3: e.tensor_tensor(
                    sl3.h[:, :, 2:Tc + 2],
                    ph.ap.rearrange("p (b t) -> p b t", b=B_LOC, t=Tc),
                    DQ_b.ap.rearrange("p (b t) -> p b t", b=B_LOC, t=Tc),
                    ALU.mult),
                    reads=(ph, DQ_b), writes=(sl3,), tag=f"b1c{ch}ev{mt}")
                p.op("vector", lambda e, mt=mt, sl3=sl3: e.tensor_copy(
                    tails[:, mt, :, :], sl3[:, :, Tc:Tc + 2]),
                    reads=(sl3,), writes=(tails,), tag=f"b1c{ch}to{mt}")
                if DBG == "h1" and mt < NPT:
                    p.op("scalar", lambda e, mt=mt, t0=t0, sl3=sl3: e.dma_start(
                        dram["out"][:, mt * 128:(mt + 1) * 128, t0:t0 + Tc]
                        .transpose([1, 0, 2]), sl3[:, :, 2:Tc + 2]),
                        reads=(sl3,), writes=(dbuf["out"],), is_dma=True,
                        is_store=True, tag="dbgh1s")
                p.op("vector", lambda e, mt=mt, sl3=sl3, ct=ct: e.tensor_scalar(
                    ct.ap, sl3[:, :, 0:Tc], par["conv_w"][:, mt, 0:1], None,
                    ALU.mult),
                    reads=(sl3, par["conv_w"]), writes=(ct,),
                    tag=f"b1c{ch}c0{mt}")
                p.op("vector", lambda e, mt=mt, sl3=sl3, ct=ct:
                     e.scalar_tensor_tensor(
                         ct.ap, sl3[:, :, 1:Tc + 1], par["conv_w"][:, mt, 1:2],
                         ct.ap, ALU.mult, ALU.add),
                     reads=(sl3, ct, par["conv_w"]), writes=(ct,),
                     tag=f"b1c{ch}c1{mt}")
                p.op("vector", lambda e, mt=mt, sl3=sl3, ct=ct:
                     e.scalar_tensor_tensor(
                         ct.ap, sl3[:, :, 2:Tc + 2], par["conv_w"][:, mt, 2:3],
                         ct.ap, ALU.mult, ALU.add),
                     reads=(sl3, ct, par["conv_w"]), writes=(ct,),
                     tag=f"b1c{ch}c2{mt}")
                p.op("scalar", lambda e, mt=mt, ct=ct: e.activation(
                    fl(hs, mt), ct.h[:].rearrange("p b t -> p (b t)"), AF.Silu,
                    bias=par["conv_b"][:, mt:mt + 1]),
                    reads=(ct, par["conv_b"]), writes=(hs,),
                    tag=f"b1c{ch}si{mt}")
                if last:
                    p.op("vector", lambda e, mt=mt, sl3=sl3: e.tensor_scalar(
                        hE32[:, mt, :, :], sl3[:, :, Tc:Tc + 1],
                        par["conv_w"][:, mt, 0:1], None, ALU.mult),
                        reads=(sl3, par["conv_w"]), writes=(hE32,),
                        tag=f"b1E0{mt}")
                    p.op("vector", lambda e, mt=mt, sl3=sl3:
                         e.scalar_tensor_tensor(
                             hE32[:, mt, :, :], sl3[:, :, Tc + 1:Tc + 2],
                             par["conv_w"][:, mt, 1:2], hE32[:, mt, :, :],
                             ALU.mult, ALU.add),
                         reads=(sl3, hE32, par["conv_w"]), writes=(hE32,),
                         tag=f"b1E1{mt}")
                    p.op("scalar", lambda e, mt=mt: e.activation(
                        hsE.h[:, mt, :, :].rearrange("p b t -> p (b t)"),
                        hE32.h[:, mt, :, :].rearrange("p b t -> p (b t)"),
                        AF.Silu, bias=par["conv_b"][:, mt:mt + 1]),
                        reads=(hE32, par["conv_b"]), writes=(hsE,),
                        tag=f"b1Es{mt}")
                if DBG == "hs" and mt < NPT:
                    p.op("vector", lambda e, mt=mt: e.tensor_copy(
                        w0.ap, fl(hs, mt)),
                        reads=(hs,), writes=(w0,), tag="dbghsc")
                    p.op("scalar", lambda e, mt=mt, t0=t0: e.dma_start(
                        dram["out"][:, mt * 128:(mt + 1) * 128, t0:t0 + Tc]
                        .transpose([1, 0, 2]),
                        w0.h[:].rearrange("p (b t) -> p b t", b=B_LOC, t=Tc)),
                        reads=(w0,), writes=(dbuf["out"],), is_dma=True,
                        is_store=True, tag="dbghss")
                p.op("scalar", lambda e, mt=mt: e.activation(
                    habs.ap, fl(hs, mt), AF.Abs),
                    reads=(hs,), writes=(habs,), tag=f"b1c{ch}ab{mt}")
                if mt == 0:
                    p.op("vector", lambda e: e.tensor_copy(fmx.ap, habs.ap),
                         reads=(habs,), writes=(fmx,), tag=f"b1c{ch}h0")
                else:
                    p.op("vector", lambda e: e.tensor_tensor(
                        fmx.ap, fmx.ap, habs.ap, ALU.max),
                        reads=(fmx, habs), writes=(fmx,), tag=f"b1c{ch}h{mt}")
            pmax_rows(None, NG, None, psT, colT, ident_f, f"b1c{ch}a2",
                      F, [(fmx, "max")])
            p.op("tensor", lambda e: e.transpose(
                psT[0:NG, 0:128], colT[:, 0:NG], ident_f.ap),
                reads=(colT, ident_f), writes=(psT,), tag=f"b1c{ch}a2t")
            p.op("vector", lambda e: e.tensor_copy(colU.ap, psT[0:NG, 0:128]),
                 reads=(psT,), writes=(colU,), tag=f"b1c{ch}a2u")
            if ch == 0:
                p.op("scalar", lambda e: e.dma_start(
                    dram["r4d"][0, :, 1:128], colU[:, 1:128]),
                    reads=(colU,), writes=(dbuf["r4d"],), is_dma=True,
                    tag=f"b1c{ch}sa2")
                p.op("vector", lambda e: e.tensor_scalar(
                    w0[0:1, 0:F], iota_h[0:1, 0:1].broadcast_to((1, F)),
                    0.0, 1.0, ALU.mult, ALU.add),
                    reads=(iota_h,), writes=(w0,), tag=f"b1c{ch}w0f")
                p.op("sync", lambda e: e.dma_start(
                    w0.h[0:1, :].rearrange("o (b j) -> o b j", b=B_LOC,
                                           j=Tc)[:, :, 1:Tc],
                    dram["r4d"][0:1, :, 1:128]),
                    reads=(dbuf["r4d"],), writes=(w0,), is_dma=True,
                    tag=f"b1c{ch}la2")
            else:
                p.op("scalar", lambda e, ch=ch: e.dma_start(
                    dram["r4d"][ch, :, 0:128], colU.ap),
                    reads=(colU,), writes=(dbuf["r4d"],), is_dma=True,
                    tag=f"b1c{ch}sa2")
                p.op("scalar", lambda e, ch=ch: e.dma_start(
                    dram["r4d"][ch - 1, :, 128:129], colU[:, 0:1],
                    allow_slow_non_contiguous=True),
                    reads=(colU,), writes=(dbuf["r4d"],), is_dma=True,
                    tag=f"b1c{ch}sa2b")
                p.op("sync", lambda e, ch=ch: e.dma_start(
                    w0.h[0:1, :].rearrange("o (b j) -> o b j", b=B_LOC, j=Tc),
                    dram["r4d"][ch:ch + 1, :, 0:128]),
                    reads=(dbuf["r4d"],), writes=(w0,), is_dma=True,
                    tag=f"b1c{ch}la2")
            p.op("vector", lambda e: e.tensor_scalar(
                habs[0:1, 0:F], w0[0:1, 0:F], 1.0 / 127.0, None, ALU.mult),
                reads=(w0,), writes=(habs,), tag=f"b1c{ch}a2a")
            p.op("vector", lambda e: e.reciprocal(habs[0:1, 0:F],
                                                  habs[0:1, 0:F]),
                 reads=(habs,), writes=(habs,), tag=f"b1c{ch}a2b")
            p.op("tensor", lambda e, habs=habs: e.matmul(
                psBC[:, 0:F], ones_f.ap, habs[0:1, 0:F], start=True, stop=True),
                reads=(habs, ones_f), writes=(psBC,), tag=f"b1c{ch}br2")
            p.op("scalar", lambda e, RS_b=RS_b: e.activation(
                RS_b.ap, psBC[:, 0:F], AF.Identity),
                reads=(psBC,), writes=(RS_b,), tag=f"b1c{ch}cr2")
            for mt in range(NHT):
                p.op("vector", lambda e, mt=mt: e.tensor_tensor(
                    w0.ap, fl(hs, mt), RS_b.ap, ALU.mult),
                    reads=(hs, RS_b), writes=(w0,), tag=f"b1c{ch}qh{mt}")
                p.op("vector", lambda e, mt=mt: e.tensor_scalar(
                    fl(hs, mt), w0.ap, magic.ap, magic.ap, ALU.add,
                    ALU.subtract),
                    reads=(w0, magic), writes=(hs,), tag=f"b1c{ch}qi{mt}")
            tlo = t0 - 1
            for mt in range(NHT):
                p.op("scalar", lambda e, mt=mt, tlo=tlo: e.dma_start(
                    dram["q2d"][:, mt * 128:(mt + 1) * 128,
                                max(tlo, 0):tlo + Tc].transpose([1, 0, 2]),
                    hs.h[:, mt, :, (1 if tlo < 0 else 0):Tc]),
                    reads=(hs,), writes=(dbuf["q2d"],), is_dma=True,
                    tag=f"b1c{ch}sq{mt}")
            if last:
                p.op("scalar", lambda e: e.activation(
                    habs[:, 0:B_LOC],
                    hsE.h[:, :, :, 0].rearrange("p a b -> p (a b)")
                    if False else hsE.h[:, 0, :, 0], AF.Abs),
                    reads=(hsE,), writes=(habs,), tag="b1Eab0")
                for mt in range(1, NHT):
                    p.op("scalar", lambda e, mt=mt: e.activation(
                        w0[:, 0:B_LOC], hsE.h[:, mt, :, 0], AF.Abs),
                        reads=(hsE,), writes=(w0,), tag=f"b1Eab{mt}")
                    p.op("vector", lambda e: e.tensor_tensor(
                        habs[:, 0:B_LOC], habs[:, 0:B_LOC], w0[:, 0:B_LOC],
                        ALU.max),
                        reads=(habs, w0), writes=(habs,), tag=f"b1Emx{mt}")
                p.op("tensor", lambda e: e.transpose(
                    psT[0:B_LOC, 0:128], habs[:, 0:B_LOC], ident_f.ap),
                    reads=(habs, ident_f), writes=(psT,), tag="b1Etr")
                p.op("vector", lambda e: e.tensor_reduce(
                    colT[0:B_LOC, 0:1], psT[0:B_LOC, 0:128], AX.X, ALU.max),
                    reads=(psT,), writes=(colT,), tag="b1Erd")
                p.op("scalar", lambda e: e.dma_start(
                    dram["r4d"][NCH, :, 0:1], colT[0:B_LOC, 0:1],
                    allow_slow_non_contiguous=True),
                    reads=(colT,), writes=(dbuf["r4d"],), is_dma=True,
                    tag="b1EsA")
                p.op("scalar", lambda e: e.dma_start(
                    dram["r4d"][NCH - 1, :, 128:129], colT[0:B_LOC, 0:1],
                    allow_slow_non_contiguous=True),
                    reads=(colT,), writes=(dbuf["r4d"],), is_dma=True,
                    tag="b1EsB")
                p.op("vector", lambda e: e.tensor_scalar(
                    colT[0:B_LOC, 0:1], colT[0:B_LOC, 0:1], 1.0 / 127.0, None,
                    ALU.mult),
                    reads=(colT,), writes=(colT,), tag="b1Ea")
                p.op("vector", lambda e: e.reciprocal(
                    colT[0:B_LOC, 0:1], colT[0:B_LOC, 0:1]),
                    reads=(colT,), writes=(colT,), tag="b1Eb")
                p.op("tensor", lambda e, colT=colT: e.transpose(
                    psT[0:1, 0:B_LOC], colT[0:B_LOC, 0:1],
                    ident_f[0:B_LOC, 0:B_LOC]),
                    reads=(colT, ident_f), writes=(psT,), tag="b1Et2")
                p.op("vector", lambda e, w0=w0: e.tensor_copy(
                    w0[0:1, 0:B_LOC], psT[0:1, 0:B_LOC]),
                    reads=(psT,), writes=(w0,), tag="b1Ev2")
                p.op("tensor", lambda e, w0=w0: e.matmul(
                    psBC[:, 0:B_LOC], ones_f.ap, w0[0:1, 0:B_LOC],
                    start=True, stop=True),
                    reads=(w0, ones_f), writes=(psBC,), tag="b1Eb2")
                p.op("scalar", lambda e, RS_b=RS_b: e.activation(
                    RS_b[:, 0:B_LOC], psBC[:, 0:B_LOC], AF.Identity),
                    reads=(psBC,), writes=(RS_b,), tag="b1El2")
                for mt in range(NHT):
                    p.op("vector", lambda e, mt=mt: e.tensor_tensor(
                        w0[:, 0:B_LOC], hsE.h[:, mt, :, 0], RS_b[:, 0:B_LOC],
                        ALU.mult),
                        reads=(hsE, RS_b), writes=(w0,), tag=f"b1Eq{mt}")
                    p.op("vector", lambda e, mt=mt: e.tensor_scalar(
                        hsE.h[:, mt, :, 0], w0[:, 0:B_LOC], magic.ap,
                        magic.ap, ALU.add, ALU.subtract),
                        reads=(w0, magic), writes=(hsE,), tag=f"b1Er{mt}")
                for mt in range(NHT):
                    p.op("scalar", lambda e, mt=mt: e.dma_start(
                        dram["q2d"][:, mt * 128:(mt + 1) * 128, T - 1:T]
                        .transpose([1, 0, 2]), hsE.h[:, mt, :, :],
                        allow_slow_non_contiguous=True),
                        reads=(hsE,), writes=(dbuf["q2d"],), is_dma=True,
                        tag=f"b1Esq{mt}")

        for ch in range(NCH):
            b1_chunk(ch, scrd[ch % 2], w0d[ch % 2], fmxd[ch % 2],
                     fmnd[ch % 2], colTd[ch % 2], colUd[ch % 2],
                     habsd[ch % 2], RS_bd[ch % 2], DQ_bd[ch % 2])

    def emit_b2(out_name):
        Tc = 128
        F = B_LOC * Tc
        NCH = T // Tc
        wst = sb("b2w_wst", [128, C])
        w2q = sb("b2w_w2q", [128, NHT, C], BF16)
        acc = sb("b2w_acc", [128, 1])
        red = sb("b2w_red", [128, 1])
        ws2c = sb("b2w_ws2c", [128, 1])
        dqw2c = sb("b2w_dqw2c", [128, 1])
        one11 = sb("b2w_one11", [128, 2])
        psT = psb("b2w_psT", [128, 128])
        fill(acc, 0.0)
        for kt in range(NHT):
            p.op("sync", lambda e, kt=kt: e.dma_start(
                wst.ap, dram["w2T"][kt * 128:(kt + 1) * 128, :]),
                reads=(dbuf["w2T"],), writes=(wst,), is_dma=True,
                tag=f"b2lw{kt}")
            p.op("vector", lambda e: e.reduce_sum(red.ap, wst.ap, AX.X,
                                                  apply_absolute_value=True),
                 reads=(wst,), writes=(red,), tag=f"b2rs{kt}")
            p.op("vector", lambda e: e.tensor_tensor(acc.ap, acc.ap, red.ap,
                                                     ALU.add),
                 reads=(acc, red), writes=(acc,), tag=f"b2ac{kt}")
        p.op("tensor", lambda e: e.transpose(psT[0:1, 0:128], acc.ap,
                                             ident_f.ap),
             reads=(acc, ident_f), writes=(psT,), tag="b2trA")
        p.op("vector", lambda e: e.reduce_sum(one11[0:1, 0:1], psT[0:1, 0:128],
                                              AX.X),
             reads=(psT,), writes=(one11,), tag="b2sumA")
        p.op("vector", lambda e: e.tensor_scalar(
            one11[0:1, 0:1], one11[0:1, 0:1], 1.0 / (C * HID), None, ALU.mult),
            reads=(one11,), writes=(one11,), tag="b2mean")
        p.op("vector", lambda e: e.reciprocal(one11[0:1, 1:2], one11[0:1, 0:1]),
             reads=(one11,), writes=(one11,), tag="b2ws")
        p.op("vector", lambda e: e.tensor_scalar(
            one11[0:1, 0:1], one11[0:1, 0:1], 1.0 / 127.0, None, ALU.mult),
            reads=(one11,), writes=(one11,), tag="b2dq")
        bcast_scalar(one11[0:1, 1:2], one11, ws2c, 2, "b2w")
        bcast_scalar(one11[0:1, 0:1], one11, dqw2c, 3, "b2d")
        for kt in range(NHT):
            p.op("sync", lambda e, kt=kt: e.dma_start(
                wst.ap, dram["w2T"][kt * 128:(kt + 1) * 128, :]),
                reads=(dbuf["w2T"],), writes=(wst,), is_dma=True,
                tag=f"b2lq{kt}")
            p.op("vector", lambda e: e.tensor_scalar(
                wst.ap, wst.ap, ws2c.ap, None, ALU.mult),
                reads=(wst, ws2c), writes=(wst,), tag=f"b2q1_{kt}")
            p.op("vector", lambda e: e.tensor_scalar(
                wst.ap, wst.ap, magic.ap, magic.ap, ALU.add, ALU.subtract),
                reads=(wst, magic), writes=(wst,), tag=f"b2q2_{kt}")
            p.op("vector", lambda e, kt=kt: e.tensor_scalar(
                w2q[:, kt, :], wst.ap, 1.0, -1.0, ALU.min, ALU.max),
                reads=(wst,), writes=(w2q,), tag=f"b2q3_{kt}")

        q2td = [sb(f"b2_q2t{i}", [128, NHT, B_LOC, Tc], BF16) for i in range(2)]
        x1td = [sb(f"b2_x1t{i}", [128, NPT, B_LOC, Tc]) for i in range(2)]
        DQ2d = [sb(f"b2_DQ2{i}", [128, F]) for i in range(2)]
        otd = [sb(f"b2_ot{i}", [128, NPT, B_LOC, Tc]) for i in range(2)]
        dqrow = [sb(f"b2_dqr{i}", [1, F]) for i in range(2)]
        psO = [psb(f"b2_psO{i}", [128, F]) for i in range(2)]
        psBC2 = psb("b2_psBC", [128, F])

        def fl(buf, pt):
            return buf.h[:, pt, :, :].rearrange("p b t -> p (b t)")

        for ch in range(NCH):
            t0 = ch * Tc
            q2t = q2td[ch % 2]
            x1t = x1td[ch % 2]
            DQ2 = DQ2d[ch % 2]
            ot = otd[ch % 2]
            for mt in range(NHT):
                p.op("sync", lambda e, mt=mt, t0=t0, q2t=q2t: e.dma_start(
                    q2t[:, mt, :, :],
                    dram["q2d"][:, mt * 128:(mt + 1) * 128, t0:t0 + Tc]
                    .transpose([1, 0, 2])),
                    reads=(dbuf["q2d"],), writes=(q2t,), is_dma=True,
                    tag=f"b2c{ch}lq{mt}")
            for pt in range(NPT):
                p.op("sync", lambda e, pt=pt, t0=t0, x1t=x1t: e.dma_start(
                    x1t[:, pt, :, :],
                    dram["x1"][:, pt * 128:(pt + 1) * 128, t0:t0 + Tc]
                    .transpose([1, 0, 2])),
                    reads=(dbuf["x1"],), writes=(x1t,), is_dma=True,
                    tag=f"b2c{ch}lx{pt}")
            dqr = dqrow[ch % 2]
            p.op("sync", lambda e, ch=ch, dqr=dqr: e.dma_start(
                dqr.h[0:1, :].rearrange("o (b j) -> o b j", b=B_LOC, j=Tc),
                dram["r4d"][ch:ch + 1, :, 1:129]),
                reads=(dbuf["r4d"],), writes=(dqr,), is_dma=True,
                tag=f"b2c{ch}ld")
            p.op("tensor", lambda e, dqr=dqr: e.matmul(
                psBC2[:, 0:F], ones_f.ap, dqr[0:1, 0:F], start=True, stop=True),
                reads=(dqr, ones_f), writes=(psBC2,), tag=f"b2c{ch}bd")
            p.op("vector", lambda e, DQ2=DQ2: e.tensor_scalar(
                DQ2.ap, psBC2[:, 0:F], dqw2c.ap, None, ALU.mult),
                reads=(psBC2, dqw2c), writes=(DQ2,), tag=f"b2c{ch}dq")
            if DBG == "q2":
                for mt in range(NPT):
                    p.op("vector", lambda e, mt=mt, ot=ot, q2t=q2t:
                         e.tensor_copy(ot[:, mt, :, :], q2t[:, mt, :, :]),
                         reads=(q2t,), writes=(ot,), tag=f"dbgq2c{mt}")
                    p.op("scalar", lambda e, mt=mt, t0=t0, ot=ot: e.dma_start(
                        dram["out"][:, mt * 128:(mt + 1) * 128, t0:t0 + Tc]
                        .transpose([1, 0, 2]), ot[:, mt, :, :]),
                        reads=(ot,), writes=(dbuf["out"],), is_dma=True,
                        is_store=True, tag=f"dbgq2s{mt}")
                continue
            if DBG == "dq2":
                p.op("scalar", lambda e, t0=t0, DQ2=DQ2: e.dma_start(
                    dram["out"][:, 0:128, t0:t0 + Tc].transpose([1, 0, 2]),
                    DQ2.h[:].rearrange("p (b t) -> p b t", b=B_LOC, t=Tc)),
                    reads=(DQ2,), writes=(dbuf["out"],), is_dma=True,
                    is_store=True, tag="dbgdq2s")
                continue
            for oc in range(NPT):
                po = psO[oc % 2]
                for kt in range(NHT):
                    p.op("tensor", lambda e, oc=oc, kt=kt, po=po, q2t=q2t: e.matmul(
                        po.ap, w2q[:, kt, oc * 128:(oc + 1) * 128],
                        fl(q2t, kt), start=(kt == 0), stop=(kt == NHT - 1)),
                        reads=(q2t, w2q), writes=(po,),
                        tag=f"b2c{ch}mm{oc}_{kt}")
                p.op("vector", lambda e, oc=oc, po=po, ot=ot, DQ2=DQ2:
                     e.tensor_tensor(
                         fl(ot, oc), po.ap, DQ2.ap, ALU.mult),
                     reads=(po, DQ2), writes=(ot,), tag=f"b2c{ch}ev{oc}")
                p.op("vector", lambda e, oc=oc, ot=ot, x1t=x1t: e.tensor_tensor(
                    fl(ot, oc), fl(ot, oc), fl(x1t, oc), ALU.add),
                    reads=(ot, x1t), writes=(ot,), tag=f"b2c{ch}rs{oc}")
            for pt in range(NPT):
                p.op("scalar", lambda e, pt=pt, t0=t0, ot=ot: e.dma_start(
                    dram[out_name][:, pt * 128:(pt + 1) * 128, t0:t0 + Tc]
                    .transpose([1, 0, 2]), ot[:, pt, :, :]),
                    reads=(ot,), writes=(dbuf[out_name],), is_dma=True,
                    is_store=(out_name == "out"), tag=f"b2c{ch}st{pt}")

    def emit_c(out_name="oc"):
        NTC = 4
        FC = T // NTC
        x2t = sb("c_x2t", [128, NPT, FC])
        xnt = [sb(f"c_xnt{i}", [128, FC]) for i in range(2)]
        xh = sb("c_xh", [128, NPT, T], BF16)
        xc = sb("c_xc", [128, NPT, T], BF16)
        xh_tm = sb("c_xhtm", [128, 16, 16, 65], BF16)
        E_buf = sb("c_E", [128, 16, FC], BF16)
        macc = sb("c_macc", [128, NPT])
        radd = sb("c_radd", [128, 1])
        hmask = sb("c_hmask", [128, NPT, 16], BF16)
        sqc = sb("c_sqc", [128, T], BF16)
        arow = sb("c_arow", [16, T])
        arow_bf = sb("c_arowbf", [16, T], BF16)
        xcld = [sb(f"c_xcl{i}", [128, T], BF16) for i in range(2)]
        xcrd = [sb(f"c_xcr{i}", [128, T], BF16) for i in range(2)]
        mxc = sb("c_mxc", [16, 1])
        zrecd = [sb(f"c_zrec{i}", [128, FC]) for i in range(2)]
        zbcd = [sb(f"c_zbc{i}", [64, FC]) for i in range(2)]
        osbd = [sb(f"c_osb{i}", [64, FC]) for i in range(2)]
        scr = {
            "xb0": sb("c_xb0", [128, FC], BF16),
            "xb1": sb("c_xb1", [128, FC], BF16),
            "sq0": sb("c_sq0", [128, FC], BF16),
            "sq1": sb("c_sq1", [128, FC], BF16),
            "m_b": sb("c_mb", [128, FC]),
            "r_b": sb("c_rb", [128, FC]),
            "v1": sb("c_v1", [128, FC]),
        }
        psA = psb("c_psA", [128, FC])
        psB = psb("c_psB", [128, FC])
        scr["psA"], scr["psB"] = psA, psB
        psC1 = [psb(f"c_psC{i}", [128, FC]) for i in range(2)]
        psPVd = [psb(f"c_psPV{i}", [128, FC]) for i in range(2)]
        psZ = psb("c_psZ", [64, FC])
        psT2 = psb("c_psT2", [128, 128], BF16)
        psD = psZ

        fill(hmask, 0.0)
        for pt in range(NPT):
            for half in range(2):
                p.op("vector",
                     lambda e, pt=pt, half=half: e.tensor_scalar(
                         hmask.h[half * 64:(half + 1) * 64, pt,
                                 2 * pt + half:2 * pt + half + 1],
                         iota_h[half * 64:(half + 1) * 64, 0:1],
                         0.0, 1.0, ALU.mult, ALU.add),
                     reads=(iota_h, hmask), writes=(hmask,),
                     tag=f"c_hm{pt}_{half}")
        fill(xh_tm, 0.0)
        p.op("vector", lambda e: e.tensor_scalar(
            xh_tm.h[:, :, :, 64:65],
            iota_h[:, 0:1].broadcast_to((128, 16, 16, 1)), 0.0, 1.0,
            ALU.mult, ALU.add),
            reads=(iota_h, xh_tm), writes=(xh_tm,), tag="c_ones")
        ones_f64 = sb("c_ones_f64", [65, 64], F32)
        p.op("vector", lambda e: e.tensor_scalar(
            ones_f64[64:65, 0:64], iota_h[64:65, 0:1].broadcast_to((1, 64)),
            0.0, 1.0, ALU.mult, ALU.add),
            reads=(iota_h,), writes=(ones_f64,), tag="c_onesf64")
        for i in range(2):
            fill(xcld[i], 0.0)
            fill(xcrd[i], 0.0)
            p.op("vector", lambda e, i=i: e.tensor_scalar(
                xcld[i][64:65, 0:T], iota_h[64:65, 0:1].broadcast_to((1, T)),
                0.0, 1.0, ALU.mult, ALU.add),
                reads=(iota_h,), writes=(xcld[i],), tag=f"c_xclone{i}")

        for b in range(B_LOC):
            fill(macc, 0.0)
            for tcx in range(NTC):
                t0 = tcx * FC
                for pt in range(NPT):
                    p.op("sync", lambda e, pt=pt, t0=t0, b=b: e.dma_start(
                        x2t[:, pt, :],
                        dram["x2"][b, pt * 128:(pt + 1) * 128, t0:t0 + FC]),
                        reads=(dbuf["x2"],), writes=(x2t,), is_dma=True,
                        tag=f"c{b}ld{tcx}_{pt}")
                emit_ln_stats(x2t, lambda pt: x2t.h[:, pt, :], FC, scr,
                              tag=f"c{b}t{tcx}ln_")
                for pt in range(NPT):
                    xn = xnt[pt % 2]
                    p.op("vector", lambda e, pt=pt, xn=xn: e.tensor_tensor(
                        xn.ap, x2t[:, pt, :], scr["m_b"].ap, ALU.subtract),
                        reads=(x2t, scr["m_b"]), writes=(xn,),
                        tag=f"c{b}t{tcx}s{pt}")
                    p.op("vector", lambda e, pt=pt, xn=xn: e.tensor_tensor(
                        xn.ap, xn.ap, scr["r_b"].ap, ALU.mult),
                        reads=(xn, scr["r_b"]), writes=(xn,),
                        tag=f"c{b}t{tcx}n{pt}")
                    p.op("vector", lambda e, pt=pt, xn=xn, t0=t0: e.tensor_copy(
                        xh.h[:, pt, t0:t0 + FC], xn.ap),
                        reads=(xn,), writes=(xh,), tag=f"c{b}t{tcx}c{pt}")
                    p.op("vector", lambda e, xn=xn: e.reduce_sum(
                        radd.ap, xn.ap, AX.X),
                        reads=(xn,), writes=(radd,), tag=f"c{b}t{tcx}r{pt}")
                    p.op("vector", lambda e, pt=pt: e.tensor_tensor(
                        macc[:, pt:pt + 1], macc[:, pt:pt + 1], radd.ap,
                        ALU.add),
                        reads=(macc, radd), writes=(macc,),
                        tag=f"c{b}t{tcx}a{pt}")
            p.op("vector", lambda e: e.tensor_scalar(
                macc.ap, macc.ap, 1.0 / T, None, ALU.mult),
                reads=(macc,), writes=(macc,), tag=f"c{b}mm")
            for pt in range(NPT):
                p.op("vector", lambda e, pt=pt: e.tensor_scalar(
                    xc.h[:, pt, :], xh.h[:, pt, :], macc[:, pt:pt + 1], None,
                    ALU.subtract),
                    reads=(xh, macc), writes=(xc,), tag=f"c{b}xc{pt}")
            for nch in range(NTC):
                for pt in range(NPT):
                    p.op("scalar", lambda e, pt=pt, nch=nch: e.activation(
                        sqc[:, nch * FC:(nch + 1) * FC],
                        xc.h[:, pt, nch * FC:(nch + 1) * FC], AF.Square),
                        reads=(xc,), writes=(sqc,), tag=f"c{b}sq{pt}_{nch}")
                    p.op("tensor", lambda e, pt=pt, nch=nch: e.matmul(
                        psD.h[0:16, :], hmask[:, pt, :],
                        sqc[:, nch * FC:(nch + 1) * FC],
                        start=(pt == 0), stop=(pt == NPT - 1)),
                        reads=(sqc, hmask), writes=(psD,),
                        tag=f"c{b}dg{pt}_{nch}")
                p.op("vector", lambda e, nch=nch: e.tensor_copy(
                    arow[0:16, nch * FC:(nch + 1) * FC], psD.h[0:16, :]),
                    reads=(psD,), writes=(arow,), tag=f"c{b}de{nch}")
            p.op("vector", lambda e: e.reduce_max(mxc.ap, arow[0:16, :], AX.X),
                 reads=(arow,), writes=(mxc,), tag=f"c{b}mx")
            p.op("vector", lambda e: e.tensor_scalar(
                arow[0:16, :], arow[0:16, :], mxc[0:16, 0:1], -0.5,
                ALU.add, ALU.mult),
                reads=(arow, mxc), writes=(arow,), tag=f"c{b}ar")
            p.op("vector", lambda e: e.tensor_copy(arow_bf[0:16, :],
                                                   arow[0:16, :]),
                 reads=(arow,), writes=(arow_bf,), tag=f"c{b}arb")
            for pt in range(NPT):
                for tt in range(16):
                    p.op("tensor", lambda e, pt=pt, tt=tt: e.transpose(
                        psT2.ap, xh.h[:, pt, tt * 128:(tt + 1) * 128],
                        ident_bf.ap),
                        reads=(xh, ident_bf), writes=(psT2,),
                        tag=f"c{b}tr{pt}_{tt}")
                    p.op("vector", lambda e, pt=pt, tt=tt: e.tensor_copy(
                        xh_tm.h[:, tt, 2 * pt, 0:64], psT2[:, 0:64]),
                        reads=(psT2,), writes=(xh_tm,),
                        tag=f"c{b}te{pt}_{tt}")
                    p.op("vector", lambda e, pt=pt, tt=tt: e.tensor_copy(
                        xh_tm.h[:, tt, 2 * pt + 1, 0:64], psT2[:, 64:128]),
                        reads=(psT2,), writes=(xh_tm,),
                        tag=f"c{b}tf{pt}_{tt}")
            for h in range(16):
                pt = h // 2
                rb = (h % 2) * 64
                xl = xcld[h % 2]
                xr = xcrd[h % 2]
                rrow = 6 + h % 2
                p.op("scalar", lambda e, h=h, rrow=rrow: e.dma_start(
                    dram["rows"][rrow:rrow + 1, 0:T // 2].bitcast(BF16),
                    arow_bf[h:h + 1, :]),
                    reads=(arow_bf,), writes=(dbuf["rows%d" % (h % 2)],),
                    is_dma=True, tag=f"c{b}h{h}as")
                p.op("sync", lambda e, rrow=rrow, xr=xr: e.dma_start(
                    xr[64:65, 0:T], dram["rows"][rrow:rrow + 1, 0:T // 2]
                    .bitcast(BF16)),
                    reads=(dbuf["rows%d" % (h % 2)],), writes=(xr,),
                    is_dma=True, tag=f"c{b}h{h}al")
                p.op("scalar", lambda e, pt=pt, rb=rb, xl=xl: e.dma_start(
                    xl[0:64, 0:T], xc.h[rb:rb + 64, pt, :]),
                    reads=(xc,), writes=(xl,), is_dma=True,
                    tag=f"c{b}h{h}xl")
                p.op("sync", lambda e, pt=pt, rb=rb, xr=xr: e.dma_start(
                    xr[0:64, 0:T], xc.h[rb:rb + 64, pt, :]),
                    reads=(xc,), writes=(xr,), is_dma=True,
                    tag=f"c{b}h{h}xr")
                for tcx in range(NTC):
                    tq0 = tcx * FC
                    psPV = psPVd[tcx % 2]
                    zrec = zrecd[tcx % 2]
                    zbc = zbcd[tcx % 2]
                    osb = osbd[tcx % 2]
                    for tt in range(16):
                        pc_ = psC1[tt % 2]
                        p.op("tensor",
                             lambda e, tt=tt, tq0=tq0, pc_=pc_, xl=xl, xr=xr:
                             e.matmul(
                                 pc_.ap, xl[0:128, tt * 128:(tt + 1) * 128],
                                 xr[0:128, tq0:tq0 + FC],
                                 start=True, stop=True),
                             reads=(xl, xr), writes=(pc_,),
                             tag=f"c{b}h{h}c{tcx}_{tt}")
                        p.op("scalar", lambda e, tt=tt, pc_=pc_: e.activation(
                            E_buf.h[:, tt, :], pc_.ap, AF.Exp),
                            reads=(pc_,), writes=(E_buf,),
                            tag=f"c{b}h{h}e{tcx}_{tt}")
                    for tt in range(16):
                        p.op("tensor", lambda e, tt=tt, h=h, psPV=psPV: e.matmul(
                            psPV[0:65, :], xh_tm.h[:, tt, h, :],
                            E_buf.h[:, tt, :],
                            start=(tt == 0), stop=(tt == 15)),
                            reads=(xh_tm, E_buf), writes=(psPV,),
                            tag=f"c{b}h{h}pv{tcx}_{tt}")
                    p.op("vector", lambda e, zrec=zrec, psPV=psPV: e.reciprocal(
                        zrec[64:65, :], psPV[64:65, :]),
                        reads=(psPV,), writes=(zrec,),
                        tag=f"c{b}h{h}z{tcx}")
                    p.op("tensor", lambda e, zrec=zrec: e.matmul(
                        psZ.ap, ones_f64[64:65, 0:64], zrec[64:65, :],
                        start=True, stop=True, tile_position=(64, 0)),
                        reads=(ones_f64, zrec), writes=(psZ,),
                        tag=f"c{b}h{h}zm{tcx}")
                    p.op("vector", lambda e, zbc=zbc: e.tensor_copy(zbc.ap, psZ.ap),
                         reads=(psZ,), writes=(zbc,), tag=f"c{b}h{h}zc{tcx}")
                    p.op("vector", lambda e, osb=osb, psPV=psPV, zbc=zbc: e.tensor_tensor(
                        osb.ap, psPV[0:64, :], zbc.ap, ALU.mult),
                        reads=(psPV, zbc), writes=(osb,),
                        tag=f"c{b}h{h}o{tcx}")
                    p.op("scalar", lambda e, b=b, h=h, tq0=tq0, osb=osb: e.dma_start(
                        dram[out_name][b, h * 64:(h + 1) * 64, tq0:tq0 + FC],
                        osb.ap),
                        reads=(osb,), writes=(dbuf[out_name],), is_dma=True,
                        is_store=(out_name == "out"),
                        tag=f"c{b}h{h}st{tcx}")

    def emit_d(out_name):
        Tc = 256
        F = B_LOC * Tc
        NCH = T // Tc
        wst = sb("d_wst", [128, C])
        Mq = sb("d_Mq", [128, NPT, C], BF16)
        aT_s = sb("d_aT", [16, C], BF16)
        lb_s = sb("d_lb", [16, C], BF16)
        acc = sb("d_acc", [128, 1])
        red = sb("d_red", [128, 1])
        ksc = sb("d_ksc", [128, 1])
        iksc = sb("d_iksc", [128, 1])
        one11 = sb("d_one11", [128, 2])
        psT = psb("d_psT", [128, 128])
        psAB = psb("d_psAB", [128, C])
        for dst, nm in ((aT_s, "aT"), (lb_s, "lora_b")):
            p.op("sync", lambda e, dst=dst, nm=nm: e.dma_start(
                wst[0:16, 0:C], dram[nm][:]),
                reads=(dbuf[nm],), writes=(wst,), is_dma=True, tag=f"d_l{nm}")
            p.op("vector", lambda e, dst=dst: e.tensor_copy(
                dst.ap, wst[0:16, 0:C]),
                reads=(wst,), writes=(dst,), tag=f"d_c{nm}")
        for kt in range(NPT):
            p.op("sync", lambda e, kt=kt: e.dma_start(
                wst.ap, dram["kern"][kt * 128:(kt + 1) * 128, :]),
                reads=(dbuf["kern"],), writes=(wst,), is_dma=True,
                tag=f"d_lk{kt}")
            p.op("vector", lambda e: e.reduce_max(
                red.ap, wst.ap, AX.X, apply_absolute_value=True),
                reads=(wst,), writes=(red,), tag=f"d_rm{kt}")
            if kt == 0:
                p.op("vector", lambda e: e.tensor_copy(acc.ap, red.ap),
                     reads=(red,), writes=(acc,), tag="d_a0")
            else:
                p.op("vector", lambda e: e.tensor_tensor(
                    acc.ap, acc.ap, red.ap, ALU.max),
                    reads=(acc, red), writes=(acc,), tag=f"d_a{kt}")
        p.op("tensor", lambda e: e.transpose(psT[0:1, 0:128], acc.ap,
                                             ident_f.ap),
             reads=(acc, ident_f), writes=(psT,), tag="d_tr")
        p.op("vector", lambda e: e.tensor_reduce(
            one11[0:1, 0:1], psT[0:1, 0:128], AX.X, ALU.max),
            reads=(psT,), writes=(one11,), tag="d_pm")
        p.op("vector", lambda e: e.tensor_scalar(
            one11[0:1, 0:1], one11[0:1, 0:1], 1.0 / 7.0, None, ALU.mult),
            reads=(one11,), writes=(one11,), tag="d_sc")
        p.op("vector", lambda e: e.reciprocal(one11[0:1, 1:2], one11[0:1, 0:1]),
             reads=(one11,), writes=(one11,), tag="d_is")
        bcast_scalar(one11[0:1, 0:1], one11, ksc, 4, "d_k")
        bcast_scalar(one11[0:1, 1:2], one11, iksc, 5, "d_i")
        for kt in range(NPT):
            p.op("sync", lambda e, kt=kt: e.dma_start(
                wst.ap, dram["kern"][kt * 128:(kt + 1) * 128, :]),
                reads=(dbuf["kern"],), writes=(wst,), is_dma=True,
                tag=f"d_lq{kt}")
            for half in range(2):
                p.op("tensor", lambda e, kt=kt, half=half: e.matmul(
                    psAB[:, half * 512:(half + 1) * 512],
                    aT_s[0:16, kt * 128:(kt + 1) * 128],
                    lb_s[0:16, half * 512:(half + 1) * 512],
                    start=True, stop=True),
                    reads=(aT_s, lb_s), writes=(psAB,), tag=f"d_ab{kt}_{half}")
            p.op("vector", lambda e: e.tensor_scalar(
                wst.ap, wst.ap, iksc.ap, None, ALU.mult),
                reads=(wst, iksc), writes=(wst,), tag=f"d_q1{kt}")
            p.op("vector", lambda e: e.tensor_scalar(
                wst.ap, wst.ap, magic.ap, magic.ap, ALU.add, ALU.subtract),
                reads=(wst, magic), writes=(wst,), tag=f"d_q2{kt}")
            p.op("vector", lambda e: e.tensor_scalar(
                wst.ap, wst.ap, ksc.ap, None, ALU.mult),
                reads=(wst, ksc), writes=(wst,), tag=f"d_q3{kt}")
            p.op("vector", lambda e, kt=kt: e.tensor_tensor(
                Mq[:, kt, :], wst.ap, psAB.ap, ALU.add),
                reads=(wst, psAB), writes=(Mq,), tag=f"d_q4{kt}")

        x2c = sb("d_x2c", [128, NPT, B_LOC, Tc])
        occ = sb("d_occ", [128, NPT, B_LOC, Tc])
        vbf = sb("d_vbf", [128, NPT, B_LOC, Tc], BF16)
        ot = sb("d_ot", [128, F])
        scr = {
            "xb0": sb("d_xb0", [128, F], BF16),
            "xb1": sb("d_xb1", [128, F], BF16),
            "sq0": sb("d_sq0", [128, F], BF16),
            "sq1": sb("d_sq1", [128, F], BF16),
            "m_b": sb("d_mb", [128, F]),
            "r_b": sb("d_rb", [128, F]),
            "v1": sb("d_v1", [128, F]),
        }
        psA2 = psb("d_psA2", [128, 512])
        psB2 = psb("d_psB2", [128, 512])
        scr["psA"], scr["psB"] = psA2, psB2
        psO = [psb(f"d_psO{i}", [128, 512]) for i in range(2)]

        def fl(buf, pt):
            return buf.h[:, pt, :, :].rearrange("p b t -> p (b t)")

        for ch in range(NCH):
            t0 = ch * Tc
            for pt in range(NPT):
                p.op("sync", lambda e, pt=pt, t0=t0: e.dma_start(
                    x2c[:, pt, :, :],
                    dram["x2"][:, pt * 128:(pt + 1) * 128, t0:t0 + Tc]
                    .transpose([1, 0, 2])),
                    reads=(dbuf["x2"],), writes=(x2c,), is_dma=True,
                    tag=f"d{ch}lx{pt}")
                p.op("sync", lambda e, pt=pt, t0=t0: e.dma_start(
                    occ[:, pt, :, :],
                    dram["oc"][:, pt * 128:(pt + 1) * 128, t0:t0 + Tc]
                    .transpose([1, 0, 2])),
                    reads=(dbuf["oc"],), writes=(occ,), is_dma=True,
                    tag=f"d{ch}lo{pt}")
            for pt in range(NPT):
                p.op("vector", lambda e, pt=pt: e.tensor_tensor(
                    fl(x2c, pt), fl(x2c, pt), fl(occ, pt), ALU.add),
                    reads=(x2c, occ), writes=(x2c,), tag=f"d{ch}x3{pt}")
            for hf in range(2):
                hsl = slice(hf * 512, (hf + 1) * 512)
                xv_ = lambda pt, hsl=hsl: x2c.h[:, pt, :, :] \
                    .rearrange("p b t -> p (b t)")[:, hsl]
                emit_ln_stats(x2c, xv_, 512, scr, tag=f"d{ch}h{hf}ln_")
                for pt in range(NPT):
                    p.op("vector", lambda e, pt=pt, hsl=hsl, xv_=xv_:
                         e.tensor_tensor(ot[:, hsl], xv_(pt), scr["m_b"][:, 0:512],
                                         ALU.subtract),
                         reads=(x2c, scr["m_b"]), writes=(ot,),
                         tag=f"d{ch}h{hf}s{pt}")
                    p.op("vector", lambda e, pt=pt, hsl=hsl: e.tensor_tensor(
                        fl(vbf, pt)[:, hsl], ot[:, hsl], scr["r_b"][:, 0:512],
                        ALU.mult),
                        reads=(ot, scr["r_b"]), writes=(vbf,),
                        tag=f"d{ch}h{hf}n{pt}")
            if DBG == "dv":
                for pt in range(NPT):
                    p.op("vector", lambda e, pt=pt: e.tensor_copy(
                        ot.ap, fl(vbf, pt)),
                        reads=(vbf,), writes=(ot,), tag="dbgdv")
                    p.op("scalar", lambda e, pt=pt, t0=t0: e.dma_start(
                        dram["out"][:, pt * 128:(pt + 1) * 128, t0:t0 + Tc]
                        .transpose([1, 0, 2]),
                        ot.h[:].rearrange("p (b t) -> p b t", b=B_LOC, t=Tc)),
                        reads=(ot,), writes=(dbuf["out"],), is_dma=True,
                        is_store=True, tag="dbgdvs")
                continue
            if DBG == "dm":
                if ch == 0:
                    for pt in range(NPT):
                        p.op("vector", lambda e, pt=pt: e.tensor_copy(
                            ot.ap, Mq.h[:, pt, 0:F]),
                            reads=(Mq,), writes=(ot,), tag="dbgdm")
                        p.op("scalar", lambda e, pt=pt: e.dma_start(
                            dram["out"][:, pt * 128:(pt + 1) * 128, 0:Tc]
                            .transpose([1, 0, 2]),
                            ot.h[:].rearrange("p (b t) -> p b t", b=B_LOC,
                                              t=Tc)),
                            reads=(ot,), writes=(dbuf["out"],), is_dma=True,
                            is_store=True, tag="dbgdms")
                continue
            for oc_ in range(NPT):
                po = psO[oc_ % 2]
                for hf in range(2):
                    hsl = slice(hf * 512, (hf + 1) * 512)
                    for kt in range(NPT):
                        p.op("tensor",
                             lambda e, oc_=oc_, kt=kt, hsl=hsl, po=po: e.matmul(
                                 po.ap, Mq[:, kt, oc_ * 128:(oc_ + 1) * 128],
                                 fl(vbf, kt)[:, hsl],
                                 start=(kt == 0), stop=(kt == NPT - 1)),
                             reads=(vbf, Mq), writes=(po,),
                             tag=f"d{ch}mm{oc_}_{hf}_{kt}")
                    p.op("scalar", lambda e, oc_=oc_, hsl=hsl, po=po:
                         e.activation(ot[:, hsl], po.ap, AF.Identity,
                                      bias=par["bias1f"][:, oc_:oc_ + 1]),
                         reads=(po, par["bias1f"]), writes=(ot,),
                         tag=f"d{ch}ev{oc_}_{hf}")
                if DBG != "do":
                    p.op("vector", lambda e, oc_=oc_: e.tensor_tensor(
                        ot.ap, ot.ap, fl(x2c, oc_), ALU.add),
                        reads=(ot, x2c), writes=(ot,), tag=f"d{ch}rs{oc_}")
                p.op("scalar", lambda e, oc_=oc_, t0=t0: e.dma_start(
                    dram[out_name][:, oc_ * 128:(oc_ + 1) * 128, t0:t0 + Tc]
                    .transpose([1, 0, 2]),
                    ot.h[:].rearrange("p (b t) -> p b t", b=B_LOC, t=Tc)),
                    reads=(ot,), writes=(dbuf[out_name],), is_dma=True,
                    is_store=(out_name == "out"), tag=f"d{ch}st{oc_}")

    STAGE_EMITTERS = {"a1": emit_a1, "a2a": emit_a2a, "a2b": emit_a2b,
                      "b1": emit_b1, "b2": emit_b2,
                      "c": emit_c, "d": emit_d}
    STAGE_OUT = {"a1": "y2", "a2a": None, "a2b": "x1",
                 "b1": None, "b2": "x2", "c": "oc", "d": "out"}
    want = [s for s in ("a1", "a2a", "a2b", "b1", "b2", "c", "d")
            if s in STAGES.split(",")]
    for i, snm in enumerate(want):
        nat = STAGE_OUT[snm]
        out_t = nat if snm != want[-1] else "out"
        with stage_scope() as st:
            _stack[0] = st
            if nat is None:
                STAGE_EMITTERS[snm]()
            else:
                STAGE_EMITTERS[snm](out_t)
            _stack[0] = None
        if i + 1 < len(want):
            p.barrier()

    p.finish()
    p.build()
    import os
    if os.environ.get("KDUMP"):
        p.dump(os.environ["KDUMP"])
    p.emit()
    return nc



def _prep_inputs(core, inputs):
    g = lambda k: np.ascontiguousarray(np.asarray(inputs[k], np.float32))
    x = np.asarray(inputs["L0_attn_x"], np.float32)
    xs = np.ascontiguousarray(
        x[core * B_LOC:(core + 1) * B_LOC].transpose(0, 2, 1))
    pc = lambda v: np.ascontiguousarray(
        np.asarray(v, np.float32).reshape(NPT, 128).T)
    return {
        "x": xs,
        "ln1_s": pc(inputs["L0_attn_ln1_s"]),
        "ln1_b": pc(inputs["L0_attn_ln1_b"]),
        "ln2_s": pc(inputs["L0_attn_ln2_s"]),
        "ln2_b": pc(inputs["L0_attn_ln2_b"]),
        "tmk": pc(inputs["L0_attn_tmk"]),
        "tmv": pc(inputs["L0_attn_tmv"]),
        "tmr": pc(inputs["L0_attn_tmr"]),
        "w_decay": pc(inputs["L0_attn_w_decay"]),
        "u_first": pc(inputs["L0_attn_u_first"]),
        "f_tmk": pc(inputs["L0_attn_f_tmk"]),
        "f_tmr": pc(inputs["L0_attn_f_tmr"]),
        "bias1f": pc(inputs["L1_ffn_bias"]),
        "conv_b": np.ascontiguousarray(
            np.asarray(inputs["L0_ffn_conv_b"], np.float32).reshape(NHT, 128).T),
        "conv_w": np.ascontiguousarray(
            np.asarray(inputs["L0_ffn_conv_w"], np.float32)[:, 0, :]
            .reshape(NHT, 128, 3).transpose(1, 0, 2)),
        "Wk": g("L0_attn_Wk"), "Wv": g("L0_attn_Wv"),
        "Wr": g("L0_attn_Wr"), "Wo": g("L0_attn_Wo"),
        "fWk": g("L0_attn_fWk"), "fWr": g("L0_attn_fWr"),
        "fWv": g("L0_attn_fWv"),
        "w1T": np.ascontiguousarray(np.asarray(inputs["L0_ffn_w1"], np.float32).T),
        "w2T": np.ascontiguousarray(np.asarray(inputs["L0_ffn_w2"], np.float32).T),
        "kern": g("L1_ffn_kernel"),
        "aT": np.ascontiguousarray(np.asarray(inputs["L1_ffn_a"], np.float32).T),
        "lora_b": g("L1_ffn_b"),
    }


def kernel(**inputs):
    nc = build_program()
    in_maps = [_prep_inputs(core, inputs) for core in range(NCORES)]
    res = run_bass_kernel_spmd(nc, in_maps, core_ids=list(range(NCORES)))
    outs = []
    for core in range(NCORES):
        o = np.asarray(res.results[core]["out"])
        outs.append(o.transpose(0, 2, 1))
    return np.ascontiguousarray(np.concatenate(outs, axis=0)).astype(np.float32)
